# version 08
# baseline (speedup 1.0000x reference)
import numpy as np
import ml_dtypes
import concourse.bass as bass
import concourse.mybir as mybir
from concourse.bass_utils import run_bass_kernel_spmd

F32 = mybir.dt.float32
BF16 = mybir.dt.bfloat16
AF = mybir.ActivationFunctionType
ALU = mybir.AluOpType
AX = mybir.AxisListType

D = 1024
DI = 512
H = 8
DH = 64
E = 14
F = 2048
T = 2048
NDT = D // 128
NFT = F // 128
LN_EPS = 1e-5
LA_CLAMP = -30.0
CHUNK_STAGE = 99
DBG = None

COMPUTE = ("tensor", "vector", "scalar")
QUEUES = ("sync", "gpsimd")
ENGINES = COMPUTE + QUEUES
SEMS = ENGINES + ("gpsimdc",)


def sem_of(eng, is_dma):
    if eng == "gpsimd" and not is_dma:
        return "gpsimdc"
    return eng


class Plan:

    def __init__(self):
        self.steps = []
        self.order = {e: [] for e in ENGINES}
        self._gid = 0

    def step(self, eng, fn, reads=(), writes=()):
        extra = tuple(r for r in reads if isinstance(r, str) and r.startswith("ps") and len(r) == 3)
        i = len(self.steps)
        self.steps.append([eng, fn, tuple(reads), tuple(writes) + extra, False, None])
        self.order[eng].append(i)
        return i

    def new_group(self):
        self._gid += 1
        return self._gid

    def dma(self, queue, out_fn, in_fn, reads=(), writes=(), group=None):
        def fn(e):
            return e.dma_start(out_fn(), in_fn())
        if group is None:
            group = self.new_group()
        i = len(self.steps)
        self.steps.append([queue, fn, tuple(reads), tuple(writes), True, group])
        self.order[queue].append(i)
        return i

    def resolve(self):
        n = len(self.steps)
        deps = [set() for _ in range(n)]
        writer = {}
        readers = {}
        for i, (eng, fn, rd, wr, is_dma, grp) in enumerate(self.steps):
            for r in rd:
                w = writer.get(r)
                if w is not None:
                    deps[i].add(w)
                readers.setdefault(r, {})[eng] = i
            for r in wr:
                w = writer.get(r)
                if w is not None:
                    deps[i].add(w)
                for e2, j in readers.get(r, {}).items():
                    deps[i].add(j)
                writer[r] = i
                readers[r] = {}
        group_last = {}
        prev_group_last = {}
        for q in QUEUES:
            prev_last = None
            runs = []
            for i in self.order[q]:
                if not self.steps[i][4]:
                    continue
                g = self.steps[i][5]
                if runs and runs[-1][0] == g:
                    runs[-1][1].append(i)
                else:
                    runs.append((g, [i]))
            for g, members in runs:
                last = members[-1]
                for m in members:
                    group_last[m] = last
                if prev_last is not None:
                    prev_group_last[members[0]] = prev_last
                prev_last = last
        step_grp = {i: self.steps[i][5] for i in range(n) if self.steps[i][4]}
        for i in range(n):
            for d in deps[i]:
                if i in step_grp and d in step_grp and d != i:
                    assert step_grp[i] != step_grp[d], (
                        "intra-group DMA dependency", i, d,
                        self.steps[i][2:4], self.steps[d][2:4])
            deps[i] = {group_last.get(d, d) for d in deps[i]}
        for first, prev_last in prev_group_last.items():
            deps[first].add(prev_last)
        needed = [False] * n
        waits = [[] for _ in range(n)]
        for i in range(n):
            eng_i = self.steps[i][0]
            for d in deps[i]:
                if d == i:
                    continue
                eng_d = self.steps[d][0]
                if eng_d == eng_i and eng_d == "tensor":
                    continue
                needed[d] = True
                waits[i].append(d)
        val = [0] * n
        cnt = {e: 0 for e in SEMS}
        inc = [0] * n
        for i, (eng, fn, rd, wr, is_dma, grp) in enumerate(self.steps):
            sm = sem_of(eng, is_dma)
            if is_dma:
                cnt[sm] += 16
                inc[i] = 16
                val[i] = cnt[sm]
            elif needed[i]:
                cnt[sm] += 1
                inc[i] = 1
                val[i] = cnt[sm]
        out = [None] * n
        lastw = {}
        for i in range(n):
            eng_i = self.steps[i][0]
            best = {}
            for d in waits[i]:
                s_d = sem_of(self.steps[d][0], self.steps[d][4])
                assert inc[d] > 0, (i, d)
                best[s_d] = max(best.get(s_d, 0), val[d])
            w = []
            for s_d, v in best.items():
                k = (eng_i, s_d)
                if lastw.get(k, 0) >= v:
                    continue
                lastw[k] = v
                w.append((s_d, v))
            out[i] = w
        return out, inc


def _consts():
    c = {}
    c["identf"] = np.eye(128, dtype=np.float32)
    s = np.arange(128)[:, None]
    t = np.arange(128)[None, :]
    same = (s // 64) == (t // 64)
    c["U64"] = ((s <= t) & same).astype(np.float32)
    c["Uc64"] = ((s > t) & same).astype(np.float32)
    c["Usum"] = np.stack([(np.arange(128) // 64) == w for w in range(2)], 1).astype(np.float32)
    c["ones1"] = np.ones((1, 128), dtype=np.float32)
    c["iota"] = np.broadcast_to(np.arange(E, dtype=np.float32), (128, E)).copy()
    c["iota_p"] = (c["iota"] + 100.0).copy()
    c["iota8"] = np.tile(np.arange(E, dtype=np.float32), (128, 8)).reshape(128, 8 * E).copy()
    c["iota8p"] = (c["iota8"] + 100.0).copy()
    c["epsc"] = np.full((128, 1), 1e-5, dtype=np.float32)
    return c


def build(n_seq=4, n_halves=None, n_experts=E, do_gla=True):
    nc = bass.Bass("TRN2", target_bir_lowering=False)
    P = Plan()

    x_d = nc.dram_tensor("L0_attn_x", [n_seq, T, D], F32, kind="ExternalInput").ap()
    Wq_d = nc.dram_tensor("L0_attn_Wq", [D, DI], F32, kind="ExternalInput").ap()
    Wk_d = nc.dram_tensor("L0_attn_Wk", [D, DI], F32, kind="ExternalInput").ap()
    Wv_d = nc.dram_tensor("L0_attn_Wv", [D, DI], F32, kind="ExternalInput").ap()
    Wa_d = nc.dram_tensor("L0_attn_Wa", [D, DI], F32, kind="ExternalInput").ap()
    ba_d = nc.dram_tensor("L0_attn_ba", [DI], F32, kind="ExternalInput").ap()
    Wo_d = nc.dram_tensor("L0_attn_Wo", [DI, D], F32, kind="ExternalInput").ap()
    gamma_d = nc.dram_tensor("L0_attn_gamma", [D], F32, kind="ExternalInput").ap()
    beta_d = nc.dram_tensor("L0_attn_beta", [D], F32, kind="ExternalInput").ap()
    rw_d = nc.dram_tensor("L0_ffn_router_w", [D, E], F32, kind="ExternalInput").ap()
    w1_d = nc.dram_tensor("L0_ffn_w1", [E, D, F], F32, kind="ExternalInput").ap()
    b1_d = nc.dram_tensor("L0_ffn_b1", [E, F], F32, kind="ExternalInput").ap()
    w2_d = nc.dram_tensor("L0_ffn_w2", [E, F, D], F32, kind="ExternalInput").ap()
    b2_d = nc.dram_tensor("L0_ffn_b2", [E, D], F32, kind="ExternalInput").ap()
    y_d = nc.dram_tensor("y", [n_seq, T, D], F32, kind="ExternalOutput").ap()
    dbg_d = nc.dram_tensor("dbg", [128, 2048], F32, kind="ExternalOutput").ap()
    w1b_d = nc.dram_tensor("w1b", [E, D, F], BF16, kind="Internal").ap()
    w2b_d = nc.dram_tensor("w2b", [E, F, D], BF16, kind="Internal").ap()


    c_d = {k: nc.inline_tensor(v, name="c_" + k).ap() for k, v in _consts().items()}

    def sbuf(name, shape, dtype):
        return nc.alloc_sbuf_tensor(name, shape, dtype)

    x1 = sbuf("x1", [128, 8 * 1024], F32)
    xn2T = sbuf("xn2T", [128, NDT * 1024], BF16)
    aw_q = sbuf("aw_q", [128, NDT * DI], BF16)
    aw_k = sbuf("aw_k", [128, NDT * DI], BF16)
    aw_v = sbuf("aw_v", [128, NDT * DI], BF16)
    aw_a = sbuf("aw_a", [128, NDT * DI], BF16)
    aw_o = sbuf("aw_o", [128, 4 * 1024], BF16)
    gamma_bc = sbuf("gamma_bc", [128, D], BF16)
    beta_bc = sbuf("beta_bc", [128, D], BF16)
    ba_bc = sbuf("ba_bc", [128, DI], F32)
    GT = sbuf("GT", [E, 1024], BF16)
    G = sbuf("G", [128, 8 * E], F32)
    b2_sb = sbuf("b2_sb", [E, D], BF16)
    b1T = sbuf("b1T", [128, NFT * E], BF16)
    rw_sb = sbuf("rw_sb", [128, NDT * E], F32)
    identf = sbuf("identf", [128, 128], F32)
    U64 = sbuf("U64", [128, 128], F32)
    Uc64 = sbuf("Uc64", [128, 128], F32)
    Usum = sbuf("Usum", [128, 2], F32)
    ones1 = sbuf("ones1", [1, 128], F32)
    stats_s = [sbuf(f"stats{i}", [128, 12], F32) for i in range(3)]
    mv_s = [sbuf(f"mv{i}", [128, 2], F32) for i in range(3)]
    rstd_s = [sbuf(f"rstd{i}", [128, 1], F32) for i in range(3)]
    col1_s = [sbuf(f"col1{i}", [128, 1], F32) for i in range(3)]
    col1 = col1_s[0]
    knorm = sbuf("knorm", [128, 2 * H], F32)
    dec2 = sbuf("dec2", [2, 2 * DI], F32)
    dec_t = sbuf("dec_t", [64, 16], F32)
    S_sb = sbuf("S_sb", [64, H * DH], F32)
    epsc = sbuf("epsc", [128, 1], F32)
    iota8 = sbuf("iota8", [128, 8 * E], F32)
    iota8p = sbuf("iota8p", [128, 8 * E], F32)
    lgh = sbuf("lgh", [128, 8 * E], F32)
    lg2 = sbuf("lg2", [128, 8 * E], F32)
    lg2b = sbuf("lg2b", [128, 8 * E], F32)
    mx8 = sbuf("mx8", [128, 8], F32)
    s8 = sbuf("s8", [128, 8], F32)
    rot0 = sbuf("rot0", [128, 128], F32)
    oT_sb = sbuf("oT_sb", [128, 4 * 128], BF16)

    AR_BYTES = 96 * 1024
    arena = nc.alloc_sbuf_tensor("arena", [128, AR_BYTES], mybir.dt.uint8)
    ar_f32 = arena.ap().bitcast(F32)
    ar_bf = arena.ap().bitcast(BF16)

    class Arena:
        def __init__(self):
            self.off = 0

        def view(self, nbytes, dtype):
            o = self.off
            assert o % 32 == 0
            self.off = (o + nbytes + 31) & ~31
            assert self.off <= AR_BYTES, (o, nbytes)
            if dtype == F32:
                return ar_f32[:, o // 4: o // 4 + nbytes // 4]
            return ar_bf[:, o // 2: o // 2 + nbytes // 2]

    fa = Arena()
    w1_sb = fa.view(NDT * 2048 * 2, BF16)
    w2_sb = fa.view(NFT * 1024 * 2, BF16)
    hid = fa.view(NFT * 1024 * 2, BF16)
    b1tmp = ar_f32[0:E, 0:F]
    grow = ar_f32[0:1, F:F + D]
    brow = ar_f32[0:1, F + D:F + 2 * D]
    barow = ar_f32[0:1, F + 2 * D:F + 2 * D + DI]

    aa = Arena()
    x_hq = aa.view(2048 * 4, F32)
    xpu = aa.view(2 * 2048 * 4, F32)
    q_sb = aa.view(2 * 512 * 4, F32)
    k_sb = aa.view(2 * 512 * 4, F32)
    v_sb = aa.view(2 * 512 * 4, F32)
    z_sb = aa.view(2 * 512 * 4, F32)
    sca = aa.view(1024 * 4, F32)
    scb = aa.view(1024 * 4, F32)
    xnT = aa.view(2048 * 2, BF16)
    qT_sb = aa.view(1024 * 4, F32)
    kT_sb = aa.view(1024 * 4, F32)
    PTm = aa.view(2 * 128 * 4, F32)
    u2 = aa.view(2 * 1024 * 4, F32)
    xn2b = aa.view(2 * 1024 * 4, F32)

    def ffn_res_range(b0, b1):
        out = []
        for dt in range(NDT):
            lo, hi = dt * 4096, (dt + 1) * 4096
            if lo < b1 and b0 < hi:
                out.append(f"w1d{dt}")
        for ft in range(NFT):
            lo, hi = 32768 + ft * 2048, 32768 + (ft + 1) * 2048
            if lo < b1 and b0 < hi:
                out.append(f"w2f{ft}")
        for ft in range(NFT):
            lo, hi = 65536 + ft * 2048, 65536 + (ft + 1) * 2048
            if lo < b1 and b0 < hi:
                out.append(f"hidf{ft}")
        return tuple(out)

    AT = {}
    off = 0

    def at_res(name, nbytes):
        nonlocal off
        AT[name] = ffn_res_range(off, off + nbytes)
        off = (off + nbytes + 31) & ~31

    at_res("x_hq0", 2048 * 4)
    at_res("xpu0", 2048 * 4)
    at_res("xpu1", 2048 * 4)
    at_res("q0", 512 * 4); at_res("q1", 512 * 4)
    at_res("k0", 512 * 4); at_res("k1", 512 * 4)
    at_res("v0", 512 * 4); at_res("v1", 512 * 4)
    at_res("z0", 512 * 4); at_res("z1", 512 * 4)
    at_res("sca", 1024 * 4)
    at_res("scb", 1024 * 4)
    at_res("xnT0", 2048 * 2)
    at_res("qT", 1024 * 4)
    at_res("kT", 1024 * 4)
    at_res("PTm", 2 * 128 * 4)
    at_res("u_t0", 1024 * 4)
    at_res("u_t1", 1024 * 4)
    at_res("xn_t0", 1024 * 4)
    at_res("xn_t1", 1024 * 4)
    assert off == aa.off, (off, aa.off)

    def R(*names):
        out = []
        for nm in names:
            out.append(nm)
            out.extend(AT.get(nm, ()))
        return tuple(out)

    pbank = [nc.alloc_psum_tensor(f"pb{i}", [128, 512], F32) for i in range(8)]
    PSN = [f"ps{i}" for i in range(8)]

    g_init_s = P.new_group()
    for name, sb in (("identf", identf), ("U64", U64), ("Uc64", Uc64),
                     ("Usum", Usum), ("ones1", ones1), ("epsc", epsc),
                     ("iota8", iota8), ("iota8p", iota8p)):
        P.dma("sync", lambda sb=sb: sb.ap(), lambda n=name: c_d[n], writes=(name,),
              group=g_init_s)
    P.dma("sync", lambda: barow, lambda: ba_d.unsqueeze(0), writes=("barow", "w1d4"), group=g_init_s)
    P.dma("sync", lambda: grow, lambda: gamma_d.unsqueeze(0), writes=("grow", "w1d2"), group=g_init_s)
    P.dma("sync", lambda: brow, lambda: beta_d.unsqueeze(0), writes=("brow", "w1d3"), group=g_init_s)
    for dt in range(NDT):
        P.dma("sync", lambda dt=dt: rw_sb.ap()[:, dt * E:(dt + 1) * E],
              lambda dt=dt: rw_d[dt * 128:(dt + 1) * 128, :], writes=(f"rw{dt}",),
              group=g_init_s)
    P.dma("sync", lambda: b1tmp, lambda: b1_d, writes=("b1tmp", "w1d0", "w1d1"), group=g_init_s)
    g_init_g = P.new_group()
    P.dma("gpsimd", lambda: b2_sb.ap(), lambda: b2_d, writes=("b2",), group=g_init_g)
    for dt in range(NDT):
        for (dst, src, nm) in ((aw_q, Wq_d, "awq"), (aw_k, Wk_d, "awk"),
                               (aw_v, Wv_d, "awv"), (aw_a, Wa_d, "awa")):
            P.dma("gpsimd", lambda dst=dst, dt=dt: dst.ap()[:, dt * DI:(dt + 1) * DI],
                  lambda src=src, dt=dt: src[dt * 128:(dt + 1) * 128, :],
                  writes=(nm + str(dt),), group=g_init_g)
    for it in range(4):
        P.dma("gpsimd", lambda it=it: aw_o.ap()[:, it * D:(it + 1) * D],
              lambda it=it: Wo_d[it * 128:(it + 1) * 128, :], writes=("awo" + str(it),),
              group=g_init_g)

    def bcast_row(row_ap, dst_ap, width, pb_i, rres, wres):
        nhalf = (width + 511) // 512
        for i in range(nhalf):
            w = min(512, width - i * 512)
            P.step("tensor", lambda e, i=i, w=w, pb_i=pb_i, row_ap=row_ap: e.matmul(
                pbank[pb_i + i].ap()[:, 0:w], ones1.ap(),
                row_ap[:, i * 512:i * 512 + w], start=True, stop=True),
                reads=("ones1",) + tuple(rres), writes=(PSN[pb_i + i],))
            P.step("vector", lambda e, i=i, w=w, pb_i=pb_i, dst_ap=dst_ap: e.tensor_copy(
                dst_ap[:, i * 512:i * 512 + w], pbank[pb_i + i].ap()[:, 0:w]),
                reads=(PSN[pb_i + i],), writes=(wres,))

    bcast_row(grow, gamma_bc.ap(), D, 0, ("grow", "w1d2"), "gamma_bc")
    bcast_row(brow, beta_bc.ap(), D, 2, ("brow", "w1d3"), "beta_bc")
    bcast_row(barow, ba_bc.ap(), DI, 4, ("barow", "w1d4"), "ba_bc")
    for ft in range(NFT):
        pb = 5 + (ft % 3)
        P.step("tensor", lambda e, ft=ft, pb=pb: e.transpose(
            pbank[pb].ap()[0:E, 0:E] if False else pbank[pb].ap()[:, 0:E],
            b1tmp[0:E, ft * 128:(ft + 1) * 128], identf.ap()[0:E, 0:E]),
            reads=("b1tmp", "w1d0", "w1d1", "identf"), writes=(PSN[pb],))
        P.step("vector", lambda e, ft=ft, pb=pb: e.tensor_copy(
            b1T.ap()[:, ft * E:(ft + 1) * E], pbank[pb].ap()[:, 0:E]),
            reads=(PSN[pb],), writes=("b1T",))

    def load_x_hq(seq, half, hq):
        buf = hq % 2
        P.dma("sync",
              lambda buf=buf: x_hq[:, buf * 2048:(buf + 1) * 2048]
              .rearrange("p (a n) -> p a n", n=D),
              lambda seq=seq, half=half, hq=hq: x_d[seq, half * 1024 + hq * 256:
                                                    half * 1024 + hq * 256 + 256, :]
              .rearrange("(a p) n -> p a n", p=128),
              writes=R(f"x_hq{buf}"))

    def attention_half(seq, half):
        def load_x_hq(seq, half, hq):
            P.dma("sync",
                  lambda: x_hq.rearrange("p (a n) -> p a n", n=D),
                  lambda seq=seq, half=half, hq=hq: x_d[seq, half * 1024 + hq * 256:
                                                        half * 1024 + hq * 256 + 256, :]
                  .rearrange("(a p) n -> p a n", p=128),
                  writes=R("x_hq0"))

        def ln_chains(par):
            xres = "x_hq0"
            xpr = f"xpu{par}"
            for tl in range(2):
                xv = x_hq[:, tl * 1024:(tl + 1) * 1024]
                xpuv = xpu[:, par * 2048 + tl * 1024: par * 2048 + (tl + 1) * 1024]
                u_t = u2[:, tl * 1024:(tl + 1) * 1024]
                xn_t = xn2b[:, tl * 1024:(tl + 1) * 1024]
                ur, xr = f"u_t{tl}", f"xn_t{tl}"
                P.step("vector", lambda e, xv=xv: e.bn_stats(stats_s[0].ap()[:, 0:6], xv[:, 0:512]),
                       reads=R(xres), writes=("stats0",))
                P.step("vector", lambda e, xv=xv: e.bn_stats(stats_s[0].ap()[:, 6:12], xv[:, 512:1024]),
                       reads=R(xres), writes=("stats0",))
                P.step("vector", lambda e: e.bn_aggr(mv_s[0].ap(), stats_s[0].ap()),
                       reads=("stats0",), writes=("mv0",))
                P.step("scalar", lambda e: e.activation(col1_s[0].ap(), mv_s[0].ap()[:, 1:2], AF.Ln, bias=epsc.ap()),
                       reads=("mv0", "epsc"), writes=("col10",))
                P.step("scalar", lambda e: e.activation(rstd_s[0].ap(), col1_s[0].ap(), AF.Exp, scale=-0.5),
                       reads=("col10",), writes=("rstd0",))
                P.step("vector", lambda e, xv=xv, u_t=u_t: e.tensor_scalar(
                    u_t, xv, mv_s[0].ap()[:, 0:1], rstd_s[0].ap(), op0=ALU.subtract, op1=ALU.mult),
                    reads=R(xres) + ("mv0", "rstd0"), writes=R(ur))
                P.step("gpsimd", lambda e, xv=xv, xpuv=xpuv, u_t=u_t: e.tensor_add(xpuv, xv, u_t),
                       reads=R(xres, ur), writes=R(xpr))
                P.step("vector", lambda e, u_t=u_t: e.bn_stats(stats_s[1].ap()[:, 0:6], u_t[:, 0:512]),
                       reads=R(ur), writes=("stats1",))
                P.step("vector", lambda e, u_t=u_t: e.bn_stats(stats_s[1].ap()[:, 6:12], u_t[:, 512:1024]),
                       reads=R(ur), writes=("stats1",))
                P.step("vector", lambda e: e.bn_aggr(mv_s[1].ap(), stats_s[1].ap()),
                       reads=("stats1",), writes=("mv1",))
                P.step("scalar", lambda e: e.activation(col1_s[1].ap(), mv_s[1].ap()[:, 1:2], AF.Ln, bias=epsc.ap()),
                       reads=("mv1", "epsc"), writes=("col11",))
                P.step("scalar", lambda e: e.activation(rstd_s[1].ap(), col1_s[1].ap(), AF.Exp, scale=-0.5),
                       reads=("col11",), writes=("rstd1",))
                P.step("vector", lambda e, u_t=u_t: e.tensor_scalar(
                    u_t, u_t, mv_s[1].ap()[:, 0:1], rstd_s[1].ap(), op0=ALU.subtract, op1=ALU.mult),
                    reads=R(ur) + ("mv1", "rstd1"), writes=R(ur))
                P.step("vector", lambda e, u_t=u_t, xn_t=xn_t: e.tensor_tensor(xn_t, u_t, gamma_bc.ap(), op=ALU.mult),
                       reads=R(ur) + ("gamma_bc",), writes=R(xr))
                P.step("vector", lambda e, xn_t=xn_t: e.tensor_tensor(xn_t, xn_t, beta_bc.ap(), op=ALU.add),
                       reads=R(xr) + ("beta_bc",), writes=R(xr))


        load_x_hq(seq, half, 0)
        ln_chains(0)
        for hq in range(4):
            xres = "x_hq0"
            xnT_v = xnT
            xnTres = "xnT0"
            for tl in range(2):
                xn_t = xn2b[:, tl * 1024:(tl + 1) * 1024]
                xr = f"xn_t{tl}"
                for dt in range(NDT):
                    pb = dt % 2
                    P.step("tensor", lambda e, dt=dt, pb=pb, xn_t=xn_t: e.transpose(
                        pbank[pb].ap()[:, 0:128], xn_t[:, dt * 128:(dt + 1) * 128], identf.ap()),
                        reads=R(xr) + ("identf",), writes=(PSN[pb],))
                    P.step("scalar", lambda e, dt=dt, tl=tl, pb=pb: e.copy(
                        xnT_v[:, dt * 256 + tl * 128: dt * 256 + tl * 128 + 128],
                        pbank[pb].ap()[:, 0:128]),
                        reads=(PSN[pb],), writes=R(xnTres))
                for dt in range(NDT):
                    lhs = xnT_v[:, dt * 256 + tl * 128: dt * 256 + tl * 128 + 128]
                    for wi, (wsb, wnm) in enumerate(((aw_q, "awq"), (aw_k, "awk"),
                                                     (aw_v, "awv"), (aw_a, "awa"))):
                        P.step("tensor", lambda e, lhs=lhs, wsb=wsb, dt=dt, wi=wi: e.matmul(
                            pbank[wi].ap(), lhs, wsb.ap()[:, dt * DI:(dt + 1) * DI],
                            start=(dt == 0), stop=(dt == NDT - 1)),
                            reads=R(xnTres) + (wnm + str(dt),), writes=(PSN[wi],))
                qv = q_sb[:, tl * 512:(tl + 1) * 512]
                kv = k_sb[:, tl * 512:(tl + 1) * 512]
                vv_ = v_sb[:, tl * 512:(tl + 1) * 512]
                zv = z_sb[:, tl * 512:(tl + 1) * 512]
                qres, kres, vres, zres = f"q{tl}", f"k{tl}", f"v{tl}", f"z{tl}"
                P.step("vector", lambda e, qv=qv: e.tensor_copy(qv, pbank[0].ap()),
                       reads=(PSN[0],), writes=R(qres))
                P.step("vector", lambda e, kv=kv: e.tensor_copy(kv, pbank[1].ap()),
                       reads=(PSN[1],), writes=R(kres))
                P.step("vector", lambda e, vv_=vv_: e.tensor_copy(vv_, pbank[2].ap()),
                       reads=(PSN[2],), writes=R(vres))
                P.step("vector", lambda e, zv=zv: e.tensor_tensor(zv, pbank[3].ap(), ba_bc.ap(), op=ALU.add),
                       reads=(PSN[3], "ba_bc"), writes=R(zres))
            ZB = R("z0", "z1")
            KB = R("k0", "k1")
            QB = R("q0", "q1")
            P.step("vector", lambda e: e.scalar_tensor_tensor(
                sca, z_sb, -1.0, z_sb, op0=ALU.mult, op1=ALU.max),
                reads=ZB, writes=R("sca"))
            P.step("vector", lambda e: e.tensor_scalar(scb, z_sb, 0.0, None, op0=ALU.min),
                   reads=ZB, writes=R("scb"))
            P.step("scalar", lambda e: e.activation(sca, sca, AF.Exp, scale=-1.0),
                   reads=R("sca"), writes=R("sca"))
            P.step("scalar", lambda e: e.activation(sca, sca, AF.Ln, bias=1.0),
                   reads=R("sca"), writes=R("sca"))
            P.step("vector", lambda e: e.tensor_sub(scb, scb, sca),
                   reads=R("sca", "scb"), writes=R("scb"))
            P.step("vector", lambda e: e.tensor_scalar(scb, scb, LA_CLAMP, None, op0=ALU.max),
                   reads=R("scb"), writes=R("scb"))
            P.step("scalar", lambda e: e.activation(sca, k_sb, AF.Square),
                   reads=KB, writes=R("sca"))
            P.step("vector", lambda e: e.tensor_reduce(
                knorm.ap(), sca.rearrange("p (g i) -> p g i", i=DH), axis=AX.X, op=ALU.add),
                reads=R("sca"), writes=("knorm",))
            P.step("scalar", lambda e: e.activation(knorm.ap(), knorm.ap(), AF.Ln),
                   reads=("knorm",), writes=("knorm",))
            P.step("scalar", lambda e: e.activation(knorm.ap(), knorm.ap(), AF.Exp, scale=0.5),
                   reads=("knorm",), writes=("knorm",))
            P.step("vector", lambda e: e.tensor_scalar(knorm.ap(), knorm.ap(), 1e-12, None, op0=ALU.max),
                   reads=("knorm",), writes=("knorm",))
            P.step("vector", lambda e: e.reciprocal(knorm.ap(), knorm.ap()),
                   reads=("knorm",), writes=("knorm",))
            P.step("vector", lambda e: e.tensor_tensor(
                k_sb.rearrange("p (g i) -> p g i", i=DH),
                k_sb.rearrange("p (g i) -> p g i", i=DH),
                knorm.ap().unsqueeze(2).broadcast_to([128, 16, DH]), op=ALU.mult),
                reads=KB + ("knorm",), writes=KB)
            for tl in range(2):
                lav = scb[:, tl * 512:(tl + 1) * 512]
                b0 = 3 * tl
                P.step("tensor", lambda e, lav=lav, b0=b0: e.matmul(
                    pbank[b0].ap(), U64.ap(), lav, start=True, stop=True),
                    reads=R("scb") + ("U64",), writes=(PSN[b0],))
                P.step("tensor", lambda e, lav=lav, b0=b0: e.matmul(
                    pbank[b0 + 1].ap(), Uc64.ap(), lav, start=True, stop=True),
                    reads=R("scb") + ("Uc64",), writes=(PSN[b0 + 1],))
                P.step("tensor", lambda e, lav=lav, b0=b0: e.matmul(
                    pbank[b0 + 2].ap()[0:2, :], Usum.ap(), lav, start=True, stop=True),
                    reads=R("scb") + ("Usum",), writes=(PSN[b0 + 2],))
            for tl in range(2):
                b0 = 3 * tl
                P.step("scalar", lambda e, tl=tl, b0=b0: e.activation(
                    sca[:, tl * 512:(tl + 1) * 512], pbank[b0].ap(), AF.Exp),
                    reads=(PSN[b0],), writes=R("sca"))
                P.step("scalar", lambda e, tl=tl, b0=b0: e.activation(
                    z_sb[:, tl * 512:(tl + 1) * 512], pbank[b0].ap(), AF.Exp, scale=-1.0),
                    reads=(PSN[b0],), writes=R(f"z{tl}"))
                P.step("scalar", lambda e, tl=tl, b0=b0: e.activation(
                    scb[:, tl * 512:(tl + 1) * 512], pbank[b0 + 1].ap(), AF.Exp),
                    reads=(PSN[b0 + 1],), writes=R("scb"))
                P.step("scalar", lambda e, tl=tl, b0=b0: e.activation(
                    dec2.ap()[0:2, tl * 512:(tl + 1) * 512], pbank[b0 + 2].ap()[0:2, :], AF.Exp),
                    reads=(PSN[b0 + 2],), writes=("dec2",))
            P.step("vector", lambda e: e.tensor_mul(q_sb, q_sb, sca),
                   reads=QB + R("sca"), writes=QB)
            P.step("vector", lambda e: e.tensor_mul(z_sb, k_sb, z_sb),
                   reads=KB + ZB, writes=ZB)
            P.step("vector", lambda e: e.tensor_mul(k_sb, k_sb, scb),
                   reads=KB + R("scb"), writes=KB)
            if hq + 1 < 4:
                load_x_hq(seq, half, hq + 1)
                ln_chains((hq + 1) % 2)
            for tl in range(2):
                ch = hq * 2 + tl
                qv = q_sb[:, tl * 512:(tl + 1) * 512]
                kiv = z_sb[:, tl * 512:(tl + 1) * 512]
                ksv = k_sb[:, tl * 512:(tl + 1) * 512]
                vv_ = v_sb[:, tl * 512:(tl + 1) * 512]
                qres, kres, vres, zres = f"q{tl}", f"k{tl}", f"v{tl}", f"z{tl}"
                for h in range(H):
                    pb = 4 + h // 4
                    P.step("tensor", lambda e, h=h, pb=pb, qv=qv: e.transpose(
                        pbank[pb].ap()[0:64, (h % 4) * 128:(h % 4) * 128 + 128],
                        qv[:, h * DH:(h + 1) * DH], identf.ap()),
                        reads=R(qres) + ("identf",), writes=(PSN[pb],))
                for h in range(H):
                    pb = 6 + h // 4
                    P.step("tensor", lambda e, h=h, pb=pb, kiv=kiv: e.transpose(
                        pbank[pb].ap()[0:64, (h % 4) * 128:(h % 4) * 128 + 128],
                        kiv[:, h * DH:(h + 1) * DH], identf.ap()),
                        reads=R(zres) + ("identf",), writes=(PSN[pb],))
                P.step("vector", lambda e: e.tensor_copy(qT_sb[0:64, 0:512], pbank[4].ap()[0:64, :]),
                       reads=(PSN[4],), writes=R("qT"))
                P.step("vector", lambda e: e.tensor_copy(qT_sb[0:64, 512:1024], pbank[5].ap()[0:64, :]),
                       reads=(PSN[5],), writes=R("qT"))
                P.step("scalar", lambda e: e.copy(kT_sb[0:64, 0:512], pbank[6].ap()[0:64, :]),
                       reads=(PSN[6],), writes=R("kT"))
                P.step("scalar", lambda e: e.copy(kT_sb[0:64, 512:1024], pbank[7].ap()[0:64, :]),
                       reads=(PSN[7],), writes=R("kT"))
                for h in range(H):
                    pb = 4 + h % 2
                    P.step("tensor", lambda e, h=h, pb=pb: e.matmul(
                        pbank[pb].ap()[:, 0:128], kT_sb[0:64, h * 128:(h + 1) * 128],
                        qT_sb[0:64, h * 128:(h + 1) * 128], start=True, stop=True),
                        reads=R("kT", "qT"), writes=(PSN[pb],))
                    P.step("vector", lambda e, h=h, pb=pb: e.tensor_tensor(
                        PTm[:, (h % 2) * 128:(h % 2) * 128 + 128], pbank[pb].ap()[:, 0:128],
                        U64.ap(), op=ALU.mult),
                        reads=(PSN[pb], "U64"), writes=R("PTm"))
                    ob = 6 + h // 4
                    P.step("tensor", lambda e, h=h, ob=ob, vv_=vv_: e.matmul(
                        pbank[ob].ap()[0:64, (h % 4) * 128:(h % 4) * 128 + 128],
                        vv_[:, h * DH:(h + 1) * DH],
                        PTm[:, (h % 2) * 128:(h % 2) * 128 + 128],
                        start=(h % 4 == 0), stop=False),
                        reads=R(vres, "PTm"), writes=(PSN[ob],))
                    P.step("tensor", lambda e, h=h, ob=ob: e.matmul(
                        pbank[ob].ap()[0:64, (h % 4) * 128:(h % 4) * 128 + 64],
                        S_sb.ap()[0:64, h * DH:(h + 1) * DH],
                        qT_sb[0:64, h * 128:h * 128 + 64], start=False, stop=False),
                        reads=("S",) + R("qT"), writes=(PSN[ob],))
                db = 2 + 3 * tl
                for h in range(H):
                    P.step("tensor", lambda e, h=h, tl=tl, db=db: e.transpose(
                        pbank[db].ap()[0:64, h * 2:h * 2 + 2],
                        dec2.ap()[0:2, tl * 512 + h * DH: tl * 512 + (h + 1) * DH],
                        identf.ap()[0:2, 0:2]),
                        reads=("dec2", "identf"), writes=(PSN[db],))
                P.step("vector", lambda e, db=db: e.tensor_copy(dec_t.ap()[0:64, 0:16], pbank[db].ap()[0:64, 0:16]),
                       reads=(PSN[db],), writes=("dec_t",))
                for h in range(H):
                    P.step("tensor", lambda e, h=h, vv_=vv_, ksv=ksv: e.matmul(
                        pbank[3].ap()[0:64, h * DH:(h + 1) * DH],
                        ksv[0:64, h * DH:(h + 1) * DH],
                        vv_[0:64, h * DH:(h + 1) * DH], start=(h == 0), stop=(h == H - 1)),
                        reads=R(kres, vres), writes=(PSN[3],))
                P.step("vector", lambda e: e.tensor_tensor(
                    S_sb.ap().rearrange("p (h i) -> p h i", i=DH),
                    S_sb.ap().rearrange("p (h i) -> p h i", i=DH),
                    dec_t.ap().rearrange("p (h w) -> p h w", w=2)[:, :, 0:1].broadcast_to([64, H, DH]),
                    op=ALU.mult),
                    reads=("S", "dec_t"), writes=("S",))
                P.step("vector", lambda e: e.tensor_add(S_sb.ap(), S_sb.ap(), pbank[3].ap()[0:64, :]),
                       reads=("S", PSN[3]), writes=("S",))
                for h in range(H):
                    ob = 6 + h // 4
                    P.step("tensor", lambda e, h=h, ob=ob: e.matmul(
                        pbank[ob].ap()[0:64, (h % 4) * 128 + 64:(h % 4) * 128 + 128],
                        S_sb.ap()[0:64, h * DH:(h + 1) * DH],
                        qT_sb[0:64, h * 128 + 64:h * 128 + 128], start=False,
                        stop=(h % 4 == 3)),
                        reads=("S",) + R("qT"), writes=(PSN[ob],))
                for h in range(H):
                    P.step("tensor", lambda e, h=h, vv_=vv_, ksv=ksv: e.matmul(
                        pbank[3].ap()[0:64, h * DH:(h + 1) * DH],
                        ksv[64:128, h * DH:(h + 1) * DH],
                        vv_[64:128, h * DH:(h + 1) * DH], start=(h == 0), stop=(h == H - 1)),
                        reads=R(kres, vres), writes=(PSN[3],))
                P.step("vector", lambda e: e.tensor_tensor(
                    S_sb.ap().rearrange("p (h i) -> p h i", i=DH),
                    S_sb.ap().rearrange("p (h i) -> p h i", i=DH),
                    dec_t.ap().rearrange("p (h w) -> p h w", w=2)[:, :, 1:2].broadcast_to([64, H, DH]),
                    op=ALU.mult),
                    reads=("S", "dec_t"), writes=("S",))
                P.step("vector", lambda e: e.tensor_add(S_sb.ap(), S_sb.ap(), pbank[3].ap()[0:64, :]),
                       reads=("S", PSN[3]), writes=("S",))
                for h in range(H):
                    ob = 6 + h // 4
                    P.step("scalar", lambda e, h=h, ob=ob: e.copy(
                        oT_sb.ap()[(h % 2) * 64:(h % 2) * 64 + 64, (h // 2) * 128:(h // 2) * 128 + 128],
                        pbank[ob].ap()[0:64, (h % 4) * 128:(h % 4) * 128 + 128]),
                        reads=(PSN[ob],), writes=("oT",))
                for it in range(4):
                    for hf in range(2):
                        pb = 4 + hf
                        P.step("tensor", lambda e, it=it, hf=hf, pb=pb: e.matmul(
                            pbank[pb].ap(), oT_sb.ap()[:, it * 128:(it + 1) * 128],
                            aw_o.ap()[:, it * D + hf * 512: it * D + hf * 512 + 512],
                            start=(it == 0), stop=(it == 3)),
                            reads=("oT", "awo" + str(it)), writes=(PSN[pb],))
                x1v = x1.ap()[:, ch * 1024:(ch + 1) * 1024]
                x1res = f"x1t{ch}"
                xpr = f"xpu{hq % 2}"
                xpuv = xpu[:, (hq % 2) * 2048 + tl * 1024: (hq % 2) * 2048 + (tl + 1) * 1024]
                P.step("vector", lambda e, x1v=x1v, xpuv=xpuv: e.tensor_add(x1v[:, 0:512], xpuv[:, 0:512], pbank[4].ap()),
                       reads=R(xpr) + (PSN[4],), writes=(x1res,))
                P.step("vector", lambda e, x1v=x1v, xpuv=xpuv: e.tensor_add(x1v[:, 512:1024], xpuv[:, 512:1024], pbank[5].ap()),
                       reads=R(xpr) + (PSN[5],), writes=(x1res,))

        def xn2_chain(ch):
            sl = 2 if ch % 2 == 0 else 0
            xn_t = xn2b[:, (ch % 2) * 1024:(ch % 2) * 1024 + 1024]
            xr = f"xn_t{ch % 2}"
            x1v = x1.ap()[:, ch * 1024:(ch + 1) * 1024]
            x1res = f"x1t{ch}"
            P.step("vector", lambda e, x1v=x1v, sl=sl: e.bn_stats(stats_s[sl].ap()[:, 0:6], x1v[:, 0:512]),
                   reads=(x1res,), writes=(f"stats{sl}",))
            P.step("vector", lambda e, x1v=x1v, sl=sl: e.bn_stats(stats_s[sl].ap()[:, 6:12], x1v[:, 512:1024]),
                   reads=(x1res,), writes=(f"stats{sl}",))
            P.step("vector", lambda e, sl=sl: e.bn_aggr(mv_s[sl].ap(), stats_s[sl].ap()),
                   reads=(f"stats{sl}",), writes=(f"mv{sl}",))
            P.step("scalar", lambda e, sl=sl: e.activation(col1_s[sl].ap(), mv_s[sl].ap()[:, 1:2], AF.Ln, bias=epsc.ap()),
                   reads=(f"mv{sl}", "epsc"), writes=(f"col1{sl}",))
            P.step("scalar", lambda e, sl=sl: e.activation(rstd_s[sl].ap(), col1_s[sl].ap(), AF.Exp, scale=-0.5),
                   reads=(f"col1{sl}",), writes=(f"rstd{sl}",))
            P.step("vector", lambda e, x1v=x1v, xn_t=xn_t, sl=sl: e.tensor_scalar(
                xn_t, x1v, mv_s[sl].ap()[:, 0:1], rstd_s[sl].ap(), op0=ALU.subtract, op1=ALU.mult),
                reads=(x1res, f"mv{sl}", f"rstd{sl}"), writes=R(xr))

        def xn2_emit(ch):
            xn_t = xn2b[:, (ch % 2) * 1024:(ch % 2) * 1024 + 1024]
            xr = f"xn_t{ch % 2}"
            for dt in range(NDT):
                pb = dt % 2
                rt = rot0.ap() if dt % 2 == 0 else sca[:, 0:128]
                rtres = "rot0" if dt % 2 == 0 else "sca"
                P.step("tensor", lambda e, dt=dt, pb=pb, xn_t=xn_t: e.transpose(
                    pbank[pb].ap()[:, 0:128], xn_t[:, dt * 128:(dt + 1) * 128], identf.ap()),
                    reads=R(xr) + ("identf",), writes=(PSN[pb],))
                P.step("scalar", lambda e, pb=pb, rt=rt: e.copy(rt, pbank[pb].ap()[:, 0:128]),
                       reads=(PSN[pb],), writes=(rtres,) if rtres == "rot0" else R(rtres))
                P.step("scalar", lambda e, dt=dt, pb=pb, ch=ch: e.copy(
                    xn2T.ap()[:, dt * 1024 + ch * 128: dt * 1024 + ch * 128 + 128],
                    pbank[pb].ap()[:, 0:128]),
                    reads=(PSN[pb],), writes=(f"xn2Tg{ch // 4}",))
                P.step("tensor", lambda e, dt=dt, rt=rt: e.matmul(
                    pbank[2].ap()[:, 0:E], rt, rw_sb.ap()[:, dt * E:(dt + 1) * E],
                    start=(dt == 0), stop=(dt == NDT - 1)),
                    reads=((rtres,) if rtres == "rot0" else R(rtres)) + (f"rw{dt}",), writes=(PSN[2],))
            P.step("vector", lambda e, ch=ch: e.tensor_copy(
                lgh.ap()[:, ch * E:(ch + 1) * E], pbank[2].ap()[:, 0:E]),
                reads=(PSN[2],), writes=("lgh",))

        xn2_chain(0)
        xn2_chain(1)
        for ch in range(8):
            xn2_emit(ch)
            if ch + 2 < 8:
                xn2_chain(ch + 2)

        L3 = lgh.ap().rearrange("p (c e) -> p c e", e=E)
        G3 = lg2.ap().rearrange("p (c e) -> p c e", e=E)
        B3 = lg2b.ap().rearrange("p (c e) -> p c e", e=E)
        mxb = mx8.ap().unsqueeze(2).broadcast_to([128, 8, E])
        s8b = s8.ap().unsqueeze(2).broadcast_to([128, 8, E])
        P.step("vector", lambda e: e.tensor_reduce(mx8.ap(), L3, axis=AX.X, op=ALU.max),
               reads=("lgh",), writes=("mx8",))
        P.step("vector", lambda e: e.tensor_tensor(G3, L3, mxb, op=ALU.subtract),
               reads=("lgh", "mx8"), writes=("lg2",))
        P.step("scalar", lambda e: e.activation(lg2b.ap(), lg2.ap(), AF.Exp),
               reads=("lg2",), writes=("lg2b",))
        P.step("vector", lambda e: e.tensor_reduce(s8.ap(), B3, axis=AX.X, op=ALU.add),
               reads=("lg2b",), writes=("s8",))
        P.step("vector", lambda e: e.reciprocal(s8.ap(), s8.ap()),
               reads=("s8",), writes=("s8",))
        P.step("vector", lambda e: e.tensor_scalar(lg2.ap(), lg2.ap(), 0.0, None, op0=ALU.is_equal),
               reads=("lg2",), writes=("lg2",))
        P.step("vector", lambda e: e.scalar_tensor_tensor(
            lg2.ap(), lg2.ap(), -100.0, iota8p.ap(), op0=ALU.mult, op1=ALU.add),
            reads=("lg2", "iota8p"), writes=("lg2",))
        P.step("vector", lambda e: e.tensor_reduce(mx8.ap(), G3, axis=AX.X, op=ALU.min),
               reads=("lg2",), writes=("mx8",))
        P.step("vector", lambda e: e.tensor_tensor(B3, iota8.ap().rearrange("p (c e) -> p c e", e=E), mxb, op=ALU.is_equal),
               reads=("iota8", "mx8"), writes=("lg2b",))
        P.step("vector", lambda e: e.tensor_tensor(
            G.ap().rearrange("p (c e) -> p c e", e=E), B3, s8b, op=ALU.mult),
            reads=("lg2b", "s8"), writes=("G",))
        for ch in range(8):
            gcol = G.ap()[:, ch * E:(ch + 1) * E]
            P.step("tensor", lambda e, gcol=gcol: e.transpose(
                pbank[3].ap()[0:E, 0:128], gcol, identf.ap()),
                reads=("G", "identf"), writes=(PSN[3],))
            P.step("vector", lambda e, ch=ch: e.tensor_copy(
                GT.ap()[0:E, ch * 128:(ch + 1) * 128], pbank[3].ap()[0:E, 0:128]),
                reads=(PSN[3],), writes=("GT",))

    def ffn_half(seq, half):
        first = (seq == 0 and half == 0)

        def load_w1(e_i, quarter=None):
            g = P.new_group()
            dts = range(NDT) if quarter is None else range(quarter * 2, quarter * 2 + 2)
            for dt in dts:
                if first:
                    P.dma("gpsimd", lambda dt=dt: w1_sb[:, dt * 2048:(dt + 1) * 2048],
                          lambda dt=dt, e_i=e_i: w1_d[e_i, dt * 128:(dt + 1) * 128, :],
                          reads=("paceF2",) if quarter is not None else (),
                          writes=(f"w1d{dt}",), group=g)
                else:
                    P.dma("gpsimd", lambda dt=dt: w1_sb[:, dt * 2048:(dt + 1) * 2048],
                          lambda dt=dt, e_i=e_i: w1b_d[e_i, dt * 128:(dt + 1) * 128, :],
                          reads=((f"w1bI{e_i}",) + (("paceF2",) if quarter is not None else ())),
                          writes=(f"w1d{dt}",), group=g)

        def load_w2(e_i, quarter=None):
            g = P.new_group()
            fts = range(NFT) if quarter is None else range(quarter * 4, quarter * 4 + 4)
            for ft in fts:
                if first:
                    P.dma("gpsimd", lambda ft=ft: w2_sb[:, ft * 1024:(ft + 1) * 1024],
                          lambda ft=ft, e_i=e_i: w2_d[e_i, ft * 128:(ft + 1) * 128, :],
                          reads=("paceF1",) if quarter is not None else (),
                          writes=(f"w2f{ft}",), group=g)
                else:
                    P.dma("sync", lambda ft=ft: w2_sb[:, ft * 1024:(ft + 1) * 1024],
                          lambda ft=ft, e_i=e_i: w2b_d[e_i, ft * 128:(ft + 1) * 128, :],
                          reads=((f"w2bI{e_i}",) + (("paceF1",) if quarter is not None else ())),
                          writes=(f"w2f{ft}",), group=g)

        def cast_images(e_i):
            P.dma("gpsimd", lambda e_i=e_i: w1b_d[e_i], lambda e_i=e_i: w1_d[e_i],
                  reads=("paceF2",), writes=(f"w1bI{e_i}",))
            P.dma("gpsimd", lambda e_i=e_i: w2b_d[e_i], lambda e_i=e_i: w2_d[e_i],
                  reads=("paceF2",), writes=(f"w2bI{e_i}",))

        load_w1(0)
        load_w2(0)
        for e_i in range(n_experts):
            for ft in range(NFT):
                for ng in range(2):
                    pb = (ft * 2 + ng) % 2
                    for dt in range(NDT):
                        P.step("tensor", lambda e, ft=ft, ng=ng, dt=dt, pb=pb: e.matmul(
                            pbank[pb].ap(),
                            w1_sb[:, dt * 2048 + ft * 128: dt * 2048 + ft * 128 + 128],
                            xn2T.ap()[:, dt * 1024 + ng * 512: dt * 1024 + ng * 512 + 512],
                            start=(dt == 0), stop=(dt == NDT - 1)),
                            reads=(f"w1d{dt}", f"xn2Tg{ng}"),
                            writes=(PSN[pb],) + (("paceF1",) if dt == NDT - 1 and ng == 1 else ()))
                    P.step("scalar", lambda e, ft=ft, ng=ng, pb=pb, e_i=e_i: e.activation(
                        hid[:, ft * 1024 + ng * 512: ft * 1024 + ng * 512 + 512],
                        pbank[pb].ap(), AF.Gelu,
                        bias=b1T.ap()[:, ft * E + e_i: ft * E + e_i + 1]),
                        reads=(PSN[pb], "b1T"), writes=(f"hidf{ft}",))

            for nt in range(8):
                pb0, pb1 = (2, 3) if nt % 2 == 0 else (4, 5)
                for hf in range(2):
                    pb = pb0 if hf == 0 else pb1
                    for ft in range(NFT):
                        P.step("tensor", lambda e, nt=nt, hf=hf, ft=ft, pb=pb: e.matmul(
                            pbank[pb].ap(),
                            hid[:, ft * 1024 + nt * 128: ft * 1024 + nt * 128 + 128],
                            w2_sb[:, ft * 1024 + hf * 512: ft * 1024 + hf * 512 + 512],
                            start=(ft == 0), stop=(ft == NFT - 1)),
                            reads=(f"hidf{ft}", f"w2f{ft}"),
                            writes=(PSN[pb],) + (("paceF2",) if ft == NFT - 1 and hf == 1 else ()))
                gres = f"x1t{nt}"
                gsc = G.ap()[:, nt * E + e_i: nt * E + e_i + 1]
                x1v = x1.ap()[:, nt * 1024:(nt + 1) * 1024]
                P.step("vector", lambda e, x1v=x1v, pb0=pb0, gsc=gsc: e.scalar_tensor_tensor(
                    x1v[:, 0:512], pbank[pb0].ap(), gsc, x1v[:, 0:512], op0=ALU.mult, op1=ALU.add),
                    reads=(PSN[pb0], "G", gres), writes=(gres,))
                P.step("vector", lambda e, x1v=x1v, pb1=pb1, gsc=gsc: e.scalar_tensor_tensor(
                    x1v[:, 512:1024], pbank[pb1].ap(), gsc, x1v[:, 512:1024], op0=ALU.mult, op1=ALU.add),
                    reads=(PSN[pb1], "G", gres), writes=(gres,))
                if e_i + 1 < E and nt % 2 == 1:
                    load_w1(e_i + 1, nt // 2)
                if first and nt == 6:
                    cast_images(e_i)
            if e_i + 1 < E:
                load_w2(e_i + 1)
        for nt in range(8):
            pb0, pb1 = (2, 3) if nt % 2 == 0 else (4, 5)
            for hf in range(2):
                pb = pb0 if hf == 0 else pb1
                P.step("tensor", lambda e, nt=nt, hf=hf, pb=pb: e.matmul(
                    pbank[pb].ap(), GT.ap()[0:E, nt * 128:(nt + 1) * 128],
                    b2_sb.ap()[0:E, hf * 512:(hf + 1) * 512], start=True, stop=True),
                    reads=("GT", "b2"), writes=(PSN[pb],))
            gres = f"x1t{nt}"
            x1v = x1.ap()[:, nt * 1024:(nt + 1) * 1024]
            P.step("vector", lambda e, x1v=x1v, pb0=pb0: e.tensor_tensor(
                x1v[:, 0:512], x1v[:, 0:512], pbank[pb0].ap(), op=ALU.add),
                reads=(gres, PSN[pb0]), writes=(gres,))
            P.step("vector", lambda e, x1v=x1v, pb1=pb1: e.tensor_tensor(
                x1v[:, 512:1024], x1v[:, 512:1024], pbank[pb1].ap(), op=ALU.add),
                reads=(gres, PSN[pb1]), writes=(gres,))
        P.dma("sync", lambda seq=seq, half=half: y_d[seq, half * 1024:(half + 1) * 1024, :]
              .rearrange("(a p) n -> p a n", p=128),
              lambda: x1.ap().rearrange("p (a n) -> p a n", n=D),
              reads=tuple(f"x1t{i}" for i in range(8)), writes=("ydone",))

    _halves = 2 * n_seq if n_halves is None else n_halves
    cnt_h = 0
    for seq in range(n_seq):
        for half in range(2):
            if cnt_h >= _halves:
                break
            cnt_h += 1
            if half == 0:
                P.step("vector", lambda e: e.memset(S_sb.ap(), 0.0),
                       reads=(), writes=("S",))
            attention_half(seq, half)
            ffn_half(seq, half)

    waits, incs = P.resolve()
    sem_guards = []
    sems = {}
    for e in SEMS:
        g = nc.semaphore("sem_" + e)
        sems[e] = g.__enter__()
        sem_guards.append(g)
    total = {e: 0 for e in SEMS}
    for i, st in enumerate(P.steps):
        total[sem_of(st[0], st[4])] += incs[i]

    blk = nc.Block()
    block = blk.__enter__()

    def mk(eng_name):
        def body(eng):
            for i in P.order[eng_name]:
                fn = P.steps[i][1]
                for (src_eng, v) in waits[i]:
                    eng.wait_ge(sems[src_eng], v)
                instr = fn(eng)
                if incs[i]:
                    instr.then_inc(sems[sem_of(eng_name, P.steps[i][4])], incs[i])
            if eng_name == "sync":
                eng.wait_ge(sems["sync"], total["sync"])
        return body

    block.sync(mk("sync"))
    block.tensor(mk("tensor"))
    block.vector(mk("vector"))
    block.scalar(mk("scalar"))
    block.gpsimd(mk("gpsimd"))
    blk.__exit__(None, None, None)
    for g in reversed(sem_guards):
        g.__exit__(None, None, None)
    return nc


def kernel(**inputs):
    n_cores = 8
    x = np.ascontiguousarray(inputs["L0_attn_x"])
    shards = np.split(x, n_cores, axis=0)
    nc = build(n_seq=shards[0].shape[0])
    common = {k: np.ascontiguousarray(np.asarray(v, dtype=np.float32))
              for k, v in inputs.items() if k != "L0_attn_x"}
    in_maps = [dict(common, L0_attn_x=np.ascontiguousarray(s)) for s in shards]
    res = run_bass_kernel_spmd(nc, in_maps, core_ids=list(range(n_cores)))
    return np.concatenate([r["y"] for r in res.results], axis=0)


# version 09
# speedup vs baseline: 1.1974x; 1.1974x over previous
import numpy as np
import ml_dtypes
import concourse.bass as bass
import concourse.mybir as mybir
from concourse.bass_utils import run_bass_kernel_spmd

F32 = mybir.dt.float32
BF16 = mybir.dt.bfloat16
AF = mybir.ActivationFunctionType
ALU = mybir.AluOpType
AX = mybir.AxisListType

D = 1024
DI = 512
H = 8
DH = 64
E = 14
F = 2048
T = 2048
NDT = D // 128
NFT = F // 128
LN_EPS = 1e-5
LA_CLAMP = -30.0
CHUNK_STAGE = 99
DBG = None

COMPUTE = ("tensor", "vector", "scalar")
QUEUES = ("sync", "gpsimd")
ENGINES = COMPUTE + QUEUES
SEMS = ENGINES + ("gpsimdc",)


def sem_of(eng, is_dma):
    if eng == "gpsimd" and not is_dma:
        return "gpsimdc"
    return eng


class Plan:

    def __init__(self):
        self.steps = []
        self.order = {e: [] for e in ENGINES}
        self._gid = 0

    def step(self, eng, fn, reads=(), writes=()):
        extra = tuple(r for r in reads if isinstance(r, str) and r.startswith("ps") and len(r) == 3)
        i = len(self.steps)
        self.steps.append([eng, fn, tuple(reads), tuple(writes) + extra, False, None])
        self.order[eng].append(i)
        return i

    def new_group(self):
        self._gid += 1
        return self._gid

    def dma(self, queue, out_fn, in_fn, reads=(), writes=(), group=None):
        def fn(e):
            return e.dma_start(out_fn(), in_fn())
        if group is None:
            group = self.new_group()
        i = len(self.steps)
        self.steps.append([queue, fn, tuple(reads), tuple(writes), True, group])
        self.order[queue].append(i)
        return i

    def resolve(self):
        n = len(self.steps)
        deps = [set() for _ in range(n)]
        writer = {}
        readers = {}
        for i, (eng, fn, rd, wr, is_dma, grp) in enumerate(self.steps):
            for r in rd:
                w = writer.get(r)
                if w is not None:
                    deps[i].add(w)
                readers.setdefault(r, {})[eng] = i
            for r in wr:
                w = writer.get(r)
                if w is not None:
                    deps[i].add(w)
                for e2, j in readers.get(r, {}).items():
                    deps[i].add(j)
                writer[r] = i
                readers[r] = {}
        group_last = {}
        prev_group_last = {}
        for q in QUEUES:
            prev_last = None
            runs = []
            for i in self.order[q]:
                if not self.steps[i][4]:
                    continue
                g = self.steps[i][5]
                if runs and runs[-1][0] == g:
                    runs[-1][1].append(i)
                else:
                    runs.append((g, [i]))
            for g, members in runs:
                last = members[-1]
                for m in members:
                    group_last[m] = last
                if prev_last is not None:
                    prev_group_last[members[0]] = prev_last
                prev_last = last
        step_grp = {i: self.steps[i][5] for i in range(n) if self.steps[i][4]}
        for i in range(n):
            for d in deps[i]:
                if i in step_grp and d in step_grp and d != i:
                    assert step_grp[i] != step_grp[d], (
                        "intra-group DMA dependency", i, d,
                        self.steps[i][2:4], self.steps[d][2:4])
            deps[i] = {group_last.get(d, d) for d in deps[i]}
        for first, prev_last in prev_group_last.items():
            deps[first].add(prev_last)
        needed = [False] * n
        waits = [[] for _ in range(n)]
        for i in range(n):
            eng_i = self.steps[i][0]
            for d in deps[i]:
                if d == i:
                    continue
                eng_d = self.steps[d][0]
                if eng_d == eng_i and eng_d == "tensor":
                    continue
                needed[d] = True
                waits[i].append(d)
        val = [0] * n
        cnt = {e: 0 for e in SEMS}
        inc = [0] * n
        for i, (eng, fn, rd, wr, is_dma, grp) in enumerate(self.steps):
            sm = sem_of(eng, is_dma)
            if is_dma:
                cnt[sm] += 16
                inc[i] = 16
                val[i] = cnt[sm]
            elif needed[i]:
                cnt[sm] += 1
                inc[i] = 1
                val[i] = cnt[sm]
        out = [None] * n
        lastw = {}
        for i in range(n):
            eng_i = self.steps[i][0]
            best = {}
            for d in waits[i]:
                s_d = sem_of(self.steps[d][0], self.steps[d][4])
                assert inc[d] > 0, (i, d)
                best[s_d] = max(best.get(s_d, 0), val[d])
            w = []
            for s_d, v in best.items():
                k = (eng_i, s_d)
                if lastw.get(k, 0) >= v:
                    continue
                lastw[k] = v
                w.append((s_d, v))
            out[i] = w
        return out, inc


def _consts():
    c = {}
    c["identf"] = np.eye(128, dtype=np.float32)
    s = np.arange(128)[:, None]
    t = np.arange(128)[None, :]
    same = (s // 64) == (t // 64)
    c["U64"] = ((s <= t) & same).astype(np.float32)
    c["Uc64"] = ((s > t) & same).astype(np.float32)
    c["Usum"] = np.stack([(np.arange(128) // 64) == w for w in range(2)], 1).astype(np.float32)
    c["ones1"] = np.ones((1, 128), dtype=np.float32)
    c["iota"] = np.broadcast_to(np.arange(E, dtype=np.float32), (128, E)).copy()
    c["iota_p"] = (c["iota"] + 100.0).copy()
    c["iota8"] = np.tile(np.arange(E, dtype=np.float32), (128, 8)).reshape(128, 8 * E).copy()
    c["iota8p"] = (c["iota8"] + 100.0).copy()
    c["epsc"] = np.full((128, 1), 1e-5, dtype=np.float32)
    return c


def build(n_seq=4, n_halves=None, n_experts=E, do_gla=True):
    nc = bass.Bass("TRN2", target_bir_lowering=False)
    P = Plan()

    x_d = nc.dram_tensor("L0_attn_x", [n_seq, T, D], F32, kind="ExternalInput").ap()
    Wq_d = nc.dram_tensor("L0_attn_Wq", [D, DI], F32, kind="ExternalInput").ap()
    Wk_d = nc.dram_tensor("L0_attn_Wk", [D, DI], F32, kind="ExternalInput").ap()
    Wv_d = nc.dram_tensor("L0_attn_Wv", [D, DI], F32, kind="ExternalInput").ap()
    Wa_d = nc.dram_tensor("L0_attn_Wa", [D, DI], F32, kind="ExternalInput").ap()
    ba_d = nc.dram_tensor("L0_attn_ba", [DI], F32, kind="ExternalInput").ap()
    Wo_d = nc.dram_tensor("L0_attn_Wo", [DI, D], F32, kind="ExternalInput").ap()
    gamma_d = nc.dram_tensor("L0_attn_gamma", [D], F32, kind="ExternalInput").ap()
    beta_d = nc.dram_tensor("L0_attn_beta", [D], F32, kind="ExternalInput").ap()
    rw_d = nc.dram_tensor("L0_ffn_router_w", [D, E], F32, kind="ExternalInput").ap()
    w1_d = nc.dram_tensor("L0_ffn_w1", [E, D, F], F32, kind="ExternalInput").ap()
    b1_d = nc.dram_tensor("L0_ffn_b1", [E, F], F32, kind="ExternalInput").ap()
    w2_d = nc.dram_tensor("L0_ffn_w2", [E, F, D], F32, kind="ExternalInput").ap()
    b2_d = nc.dram_tensor("L0_ffn_b2", [E, D], F32, kind="ExternalInput").ap()
    y_d = nc.dram_tensor("y", [n_seq, T, D], F32, kind="ExternalOutput").ap()
    dbg_d = nc.dram_tensor("dbg", [128, 2048], F32, kind="ExternalOutput").ap()


    c_d = {k: nc.inline_tensor(v, name="c_" + k).ap() for k, v in _consts().items()}

    def sbuf(name, shape, dtype):
        return nc.alloc_sbuf_tensor(name, shape, dtype)

    x1 = sbuf("x1", [128, 8 * 1024], F32)
    xn2T = sbuf("xn2T", [128, NDT * 1024], BF16)
    aw_q = sbuf("aw_q", [128, NDT * DI], BF16)
    aw_k = sbuf("aw_k", [128, NDT * DI], BF16)
    aw_v = sbuf("aw_v", [128, NDT * DI], BF16)
    aw_a = sbuf("aw_a", [128, NDT * DI], BF16)
    aw_o = sbuf("aw_o", [128, 4 * 1024], BF16)
    gamma_bc = sbuf("gamma_bc", [128, D], BF16)
    beta_bc = sbuf("beta_bc", [128, D], BF16)
    ba_bc = sbuf("ba_bc", [128, DI], F32)
    GT = sbuf("GT", [E, 1024], BF16)
    G = sbuf("G", [128, 8 * E], F32)
    b2_sb = sbuf("b2_sb", [E, D], BF16)
    b1T = sbuf("b1T", [128, NFT * E], BF16)
    rw_sb = sbuf("rw_sb", [128, NDT * E], F32)
    identf = sbuf("identf", [128, 128], F32)
    U64 = sbuf("U64", [128, 128], F32)
    Uc64 = sbuf("Uc64", [128, 128], F32)
    Usum = sbuf("Usum", [128, 2], F32)
    ones1 = sbuf("ones1", [1, 128], F32)
    stats_s = [sbuf(f"stats{i}", [128, 12], F32) for i in range(3)]
    mv_s = [sbuf(f"mv{i}", [128, 2], F32) for i in range(3)]
    rstd_s = [sbuf(f"rstd{i}", [128, 1], F32) for i in range(3)]
    col1_s = [sbuf(f"col1{i}", [128, 1], F32) for i in range(3)]
    col1 = col1_s[0]
    knorm = sbuf("knorm", [128, 2 * H], F32)
    dec2 = sbuf("dec2", [2, 2 * DI], F32)
    dec_t = sbuf("dec_t", [64, 16], F32)
    S_sb = sbuf("S_sb", [64, H * DH], F32)
    epsc = sbuf("epsc", [128, 1], F32)
    iota8 = sbuf("iota8", [128, 8 * E], F32)
    iota8p = sbuf("iota8p", [128, 8 * E], F32)
    lgh = sbuf("lgh", [128, 8 * E], F32)
    lg2 = sbuf("lg2", [128, 8 * E], F32)
    lg2b = sbuf("lg2b", [128, 8 * E], F32)
    mx8 = sbuf("mx8", [128, 8], F32)
    s8 = sbuf("s8", [128, 8], F32)
    rot0 = sbuf("rot0", [128, 128], F32)
    oT_sb = sbuf("oT_sb", [128, 4 * 128], BF16)

    AR_BYTES = 96 * 1024
    arena = nc.alloc_sbuf_tensor("arena", [128, AR_BYTES], mybir.dt.uint8)
    ar_f32 = arena.ap().bitcast(F32)
    ar_bf = arena.ap().bitcast(BF16)

    class Arena:
        def __init__(self):
            self.off = 0

        def view(self, nbytes, dtype):
            o = self.off
            assert o % 32 == 0
            self.off = (o + nbytes + 31) & ~31
            assert self.off <= AR_BYTES, (o, nbytes)
            if dtype == F32:
                return ar_f32[:, o // 4: o // 4 + nbytes // 4]
            return ar_bf[:, o // 2: o // 2 + nbytes // 2]

    fa = Arena()
    w1_sb = fa.view(NDT * 2048 * 2, BF16)
    w2_sb = fa.view(NFT * 1024 * 2, BF16)
    hid = fa.view(NFT * 1024 * 2, BF16)
    b1tmp = ar_f32[0:E, 0:F]
    grow = ar_f32[0:1, F:F + D]
    brow = ar_f32[0:1, F + D:F + 2 * D]
    barow = ar_f32[0:1, F + 2 * D:F + 2 * D + DI]

    aa = Arena()
    x_hq = aa.view(2048 * 4, F32)
    xpu = aa.view(2048 * 4, F32)
    q_sb = aa.view(2 * 512 * 4, F32)
    k_sb = aa.view(2 * 512 * 4, F32)
    v_sb = aa.view(2 * 512 * 4, F32)
    z_sb = aa.view(2 * 512 * 4, F32)
    sca = aa.view(1024 * 4, F32)
    scb = aa.view(1024 * 4, F32)
    xnT = aa.view(2048 * 2, BF16)
    qT_sb = aa.view(1024 * 4, F32)
    kT_sb = aa.view(1024 * 4, F32)
    PTm = aa.view(2 * 128 * 4, F32)
    u2 = aa.view(2 * 1024 * 4, F32)
    xn2b = aa.view(2 * 1024 * 4, F32)

    def ffn_res_range(b0, b1):
        out = []
        for dt in range(NDT):
            lo, hi = dt * 4096, (dt + 1) * 4096
            if lo < b1 and b0 < hi:
                out.append(f"w1d{dt}")
        for ft in range(NFT):
            lo, hi = 32768 + ft * 2048, 32768 + (ft + 1) * 2048
            if lo < b1 and b0 < hi:
                out.append(f"w2f{ft}")
        for ft in range(NFT):
            lo, hi = 65536 + ft * 2048, 65536 + (ft + 1) * 2048
            if lo < b1 and b0 < hi:
                out.append(f"hidf{ft}")
        return tuple(out)

    AT = {}
    off = 0

    def at_res(name, nbytes):
        nonlocal off
        AT[name] = ffn_res_range(off, off + nbytes)
        off = (off + nbytes + 31) & ~31

    at_res("x_hq0", 2048 * 4)
    at_res("xpu", 2048 * 4)
    at_res("q0", 512 * 4); at_res("q1", 512 * 4)
    at_res("k0", 512 * 4); at_res("k1", 512 * 4)
    at_res("v0", 512 * 4); at_res("v1", 512 * 4)
    at_res("z0", 512 * 4); at_res("z1", 512 * 4)
    at_res("sca", 1024 * 4)
    at_res("scb", 1024 * 4)
    at_res("xnT0", 2048 * 2)
    at_res("qT", 1024 * 4)
    at_res("kT", 1024 * 4)
    at_res("PTm", 2 * 128 * 4)
    at_res("u_t0", 1024 * 4)
    at_res("u_t1", 1024 * 4)
    at_res("xn_t0", 1024 * 4)
    at_res("xn_t1", 1024 * 4)
    assert off == aa.off, (off, aa.off)

    def R(*names):
        out = []
        for nm in names:
            out.append(nm)
            out.extend(AT.get(nm, ()))
        return tuple(out)

    pbank = [nc.alloc_psum_tensor(f"pb{i}", [128, 512], F32) for i in range(8)]
    PSN = [f"ps{i}" for i in range(8)]

    g_init_s = P.new_group()
    for name, sb in (("identf", identf), ("U64", U64), ("Uc64", Uc64),
                     ("Usum", Usum), ("ones1", ones1), ("epsc", epsc),
                     ("iota8", iota8), ("iota8p", iota8p)):
        P.dma("sync", lambda sb=sb: sb.ap(), lambda n=name: c_d[n], writes=(name,),
              group=g_init_s)
    P.dma("sync", lambda: barow, lambda: ba_d.unsqueeze(0), writes=("barow", "w1d4"), group=g_init_s)
    P.dma("sync", lambda: grow, lambda: gamma_d.unsqueeze(0), writes=("grow", "w1d2"), group=g_init_s)
    P.dma("sync", lambda: brow, lambda: beta_d.unsqueeze(0), writes=("brow", "w1d3"), group=g_init_s)
    for dt in range(NDT):
        P.dma("sync", lambda dt=dt: rw_sb.ap()[:, dt * E:(dt + 1) * E],
              lambda dt=dt: rw_d[dt * 128:(dt + 1) * 128, :], writes=(f"rw{dt}",),
              group=g_init_s)
    P.dma("sync", lambda: b1tmp, lambda: b1_d, writes=("b1tmp", "w1d0", "w1d1"), group=g_init_s)
    g_init_g = P.new_group()
    P.dma("gpsimd", lambda: b2_sb.ap(), lambda: b2_d, writes=("b2",), group=g_init_g)
    for dt in range(NDT):
        for (dst, src, nm) in ((aw_q, Wq_d, "awq"), (aw_k, Wk_d, "awk"),
                               (aw_v, Wv_d, "awv"), (aw_a, Wa_d, "awa")):
            P.dma("gpsimd", lambda dst=dst, dt=dt: dst.ap()[:, dt * DI:(dt + 1) * DI],
                  lambda src=src, dt=dt: src[dt * 128:(dt + 1) * 128, :],
                  writes=(nm + str(dt),), group=g_init_g)
    for it in range(4):
        P.dma("gpsimd", lambda it=it: aw_o.ap()[:, it * D:(it + 1) * D],
              lambda it=it: Wo_d[it * 128:(it + 1) * 128, :], writes=("awo" + str(it),),
              group=g_init_g)

    def bcast_row(row_ap, dst_ap, width, pb_i, rres, wres):
        nhalf = (width + 511) // 512
        for i in range(nhalf):
            w = min(512, width - i * 512)
            P.step("tensor", lambda e, i=i, w=w, pb_i=pb_i, row_ap=row_ap: e.matmul(
                pbank[pb_i + i].ap()[:, 0:w], ones1.ap(),
                row_ap[:, i * 512:i * 512 + w], start=True, stop=True),
                reads=("ones1",) + tuple(rres), writes=(PSN[pb_i + i],))
            P.step("vector", lambda e, i=i, w=w, pb_i=pb_i, dst_ap=dst_ap: e.tensor_copy(
                dst_ap[:, i * 512:i * 512 + w], pbank[pb_i + i].ap()[:, 0:w]),
                reads=(PSN[pb_i + i],), writes=(wres,))

    bcast_row(grow, gamma_bc.ap(), D, 0, ("grow", "w1d2"), "gamma_bc")
    bcast_row(brow, beta_bc.ap(), D, 2, ("brow", "w1d3"), "beta_bc")
    bcast_row(barow, ba_bc.ap(), DI, 4, ("barow", "w1d4"), "ba_bc")
    for ft in range(NFT):
        pb = 5 + (ft % 3)
        P.step("tensor", lambda e, ft=ft, pb=pb: e.transpose(
            pbank[pb].ap()[0:E, 0:E] if False else pbank[pb].ap()[:, 0:E],
            b1tmp[0:E, ft * 128:(ft + 1) * 128], identf.ap()[0:E, 0:E]),
            reads=("b1tmp", "w1d0", "w1d1", "identf"), writes=(PSN[pb],))
        P.step("vector", lambda e, ft=ft, pb=pb: e.tensor_copy(
            b1T.ap()[:, ft * E:(ft + 1) * E], pbank[pb].ap()[:, 0:E]),
            reads=(PSN[pb],), writes=("b1T",))

    def load_x_hq(seq, half, hq):
        buf = hq % 2
        P.dma("sync",
              lambda buf=buf: x_hq[:, buf * 2048:(buf + 1) * 2048]
              .rearrange("p (a n) -> p a n", n=D),
              lambda seq=seq, half=half, hq=hq: x_d[seq, half * 1024 + hq * 256:
                                                    half * 1024 + hq * 256 + 256, :]
              .rearrange("(a p) n -> p a n", p=128),
              writes=R(f"x_hq{buf}"))

    def attention_half(seq, half):
        def load_x_hq(seq, half, hq):
            P.dma("sync",
                  lambda: x_hq.rearrange("p (a n) -> p a n", n=D),
                  lambda seq=seq, half=half, hq=hq: x_d[seq, half * 1024 + hq * 256:
                                                        half * 1024 + hq * 256 + 256, :]
                  .rearrange("(a p) n -> p a n", p=128),
                  writes=R("x_hq0"))

        load_x_hq(seq, half, 0)
        for hq in range(4):
            xres = "x_hq0"
            xnT_v = xnT
            xnTres = "xnT0"
            for tl in range(2):
                xv = x_hq[:, tl * 1024:(tl + 1) * 1024]
                xpuv = xpu[:, tl * 1024:(tl + 1) * 1024]
                u_t = u2[:, tl * 1024:(tl + 1) * 1024]
                xn_t = xn2b[:, tl * 1024:(tl + 1) * 1024]
                ur, xr = f"u_t{tl}", f"xn_t{tl}"
                P.step("vector", lambda e, xv=xv: e.bn_stats(stats_s[0].ap()[:, 0:6], xv[:, 0:512]),
                       reads=R(xres), writes=("stats0",))
                P.step("vector", lambda e, xv=xv: e.bn_stats(stats_s[0].ap()[:, 6:12], xv[:, 512:1024]),
                       reads=R(xres), writes=("stats0",))
                P.step("vector", lambda e: e.bn_aggr(mv_s[0].ap(), stats_s[0].ap()),
                       reads=("stats0",), writes=("mv0",))
                P.step("scalar", lambda e: e.activation(col1_s[0].ap(), mv_s[0].ap()[:, 1:2], AF.Ln, bias=epsc.ap()),
                       reads=("mv0", "epsc"), writes=("col10",))
                P.step("scalar", lambda e: e.activation(rstd_s[0].ap(), col1_s[0].ap(), AF.Exp, scale=-0.5),
                       reads=("col10",), writes=("rstd0",))
                P.step("vector", lambda e, xv=xv, u_t=u_t: e.tensor_scalar(
                    u_t, xv, mv_s[0].ap()[:, 0:1], rstd_s[0].ap(), op0=ALU.subtract, op1=ALU.mult),
                    reads=R(xres) + ("mv0", "rstd0"), writes=R(ur))
                P.step("gpsimd", lambda e, xv=xv, xpuv=xpuv, u_t=u_t: e.tensor_add(xpuv, xv, u_t),
                       reads=R(xres, ur), writes=R("xpu"))
                P.step("vector", lambda e, u_t=u_t: e.bn_stats(stats_s[1].ap()[:, 0:6], u_t[:, 0:512]),
                       reads=R(ur), writes=("stats1",))
                P.step("vector", lambda e, u_t=u_t: e.bn_stats(stats_s[1].ap()[:, 6:12], u_t[:, 512:1024]),
                       reads=R(ur), writes=("stats1",))
                P.step("vector", lambda e: e.bn_aggr(mv_s[1].ap(), stats_s[1].ap()),
                       reads=("stats1",), writes=("mv1",))
                P.step("scalar", lambda e: e.activation(col1_s[1].ap(), mv_s[1].ap()[:, 1:2], AF.Ln, bias=epsc.ap()),
                       reads=("mv1", "epsc"), writes=("col11",))
                P.step("scalar", lambda e: e.activation(rstd_s[1].ap(), col1_s[1].ap(), AF.Exp, scale=-0.5),
                       reads=("col11",), writes=("rstd1",))
                P.step("vector", lambda e, u_t=u_t: e.tensor_scalar(
                    u_t, u_t, mv_s[1].ap()[:, 0:1], rstd_s[1].ap(), op0=ALU.subtract, op1=ALU.mult),
                    reads=R(ur) + ("mv1", "rstd1"), writes=R(ur))
                P.step("vector", lambda e, u_t=u_t, xn_t=xn_t: e.tensor_tensor(xn_t, u_t, gamma_bc.ap(), op=ALU.mult),
                       reads=R(ur) + ("gamma_bc",), writes=R(xr))
                P.step("vector", lambda e, xn_t=xn_t: e.tensor_tensor(xn_t, xn_t, beta_bc.ap(), op=ALU.add),
                       reads=R(xr) + ("beta_bc",), writes=R(xr))
            if hq + 1 < 4:
                load_x_hq(seq, half, hq + 1)
            for tl in range(2):
                xn_t = xn2b[:, tl * 1024:(tl + 1) * 1024]
                xr = f"xn_t{tl}"
                for dt in range(NDT):
                    pb = dt % 2
                    P.step("tensor", lambda e, dt=dt, pb=pb, xn_t=xn_t: e.transpose(
                        pbank[pb].ap()[:, 0:128], xn_t[:, dt * 128:(dt + 1) * 128], identf.ap()),
                        reads=R(xr) + ("identf",), writes=(PSN[pb],))
                    P.step("scalar", lambda e, dt=dt, tl=tl, pb=pb: e.copy(
                        xnT_v[:, dt * 256 + tl * 128: dt * 256 + tl * 128 + 128],
                        pbank[pb].ap()[:, 0:128]),
                        reads=(PSN[pb],), writes=R(xnTres))
                for dt in range(NDT):
                    lhs = xnT_v[:, dt * 256 + tl * 128: dt * 256 + tl * 128 + 128]
                    for wi, (wsb, wnm) in enumerate(((aw_q, "awq"), (aw_k, "awk"),
                                                     (aw_v, "awv"), (aw_a, "awa"))):
                        P.step("tensor", lambda e, lhs=lhs, wsb=wsb, dt=dt, wi=wi: e.matmul(
                            pbank[wi].ap(), lhs, wsb.ap()[:, dt * DI:(dt + 1) * DI],
                            start=(dt == 0), stop=(dt == NDT - 1)),
                            reads=R(xnTres) + (wnm + str(dt),), writes=(PSN[wi],))
                qv = q_sb[:, tl * 512:(tl + 1) * 512]
                kv = k_sb[:, tl * 512:(tl + 1) * 512]
                vv_ = v_sb[:, tl * 512:(tl + 1) * 512]
                zv = z_sb[:, tl * 512:(tl + 1) * 512]
                qres, kres, vres, zres = f"q{tl}", f"k{tl}", f"v{tl}", f"z{tl}"
                P.step("vector", lambda e, qv=qv: e.tensor_copy(qv, pbank[0].ap()),
                       reads=(PSN[0],), writes=R(qres))
                P.step("vector", lambda e, kv=kv: e.tensor_copy(kv, pbank[1].ap()),
                       reads=(PSN[1],), writes=R(kres))
                P.step("vector", lambda e, vv_=vv_: e.tensor_copy(vv_, pbank[2].ap()),
                       reads=(PSN[2],), writes=R(vres))
                P.step("vector", lambda e, zv=zv: e.tensor_tensor(zv, pbank[3].ap(), ba_bc.ap(), op=ALU.add),
                       reads=(PSN[3], "ba_bc"), writes=R(zres))
            ZB = R("z0", "z1")
            KB = R("k0", "k1")
            QB = R("q0", "q1")
            P.step("vector", lambda e: e.scalar_tensor_tensor(
                sca, z_sb, -1.0, z_sb, op0=ALU.mult, op1=ALU.max),
                reads=ZB, writes=R("sca"))
            P.step("vector", lambda e: e.tensor_scalar(scb, z_sb, 0.0, None, op0=ALU.min),
                   reads=ZB, writes=R("scb"))
            P.step("scalar", lambda e: e.activation(sca, sca, AF.Exp, scale=-1.0),
                   reads=R("sca"), writes=R("sca"))
            P.step("scalar", lambda e: e.activation(sca, sca, AF.Ln, bias=1.0),
                   reads=R("sca"), writes=R("sca"))
            P.step("vector", lambda e: e.tensor_sub(scb, scb, sca),
                   reads=R("sca", "scb"), writes=R("scb"))
            P.step("vector", lambda e: e.tensor_scalar(scb, scb, LA_CLAMP, None, op0=ALU.max),
                   reads=R("scb"), writes=R("scb"))
            P.step("scalar", lambda e: e.activation(sca, k_sb, AF.Square),
                   reads=KB, writes=R("sca"))
            P.step("vector", lambda e: e.tensor_reduce(
                knorm.ap(), sca.rearrange("p (g i) -> p g i", i=DH), axis=AX.X, op=ALU.add),
                reads=R("sca"), writes=("knorm",))
            P.step("scalar", lambda e: e.activation(knorm.ap(), knorm.ap(), AF.Ln),
                   reads=("knorm",), writes=("knorm",))
            P.step("scalar", lambda e: e.activation(knorm.ap(), knorm.ap(), AF.Exp, scale=0.5),
                   reads=("knorm",), writes=("knorm",))
            P.step("vector", lambda e: e.tensor_scalar(knorm.ap(), knorm.ap(), 1e-12, None, op0=ALU.max),
                   reads=("knorm",), writes=("knorm",))
            P.step("vector", lambda e: e.reciprocal(knorm.ap(), knorm.ap()),
                   reads=("knorm",), writes=("knorm",))
            P.step("vector", lambda e: e.tensor_tensor(
                k_sb.rearrange("p (g i) -> p g i", i=DH),
                k_sb.rearrange("p (g i) -> p g i", i=DH),
                knorm.ap().unsqueeze(2).broadcast_to([128, 16, DH]), op=ALU.mult),
                reads=KB + ("knorm",), writes=KB)
            for tl in range(2):
                lav = scb[:, tl * 512:(tl + 1) * 512]
                b0 = 3 * tl
                P.step("tensor", lambda e, lav=lav, b0=b0: e.matmul(
                    pbank[b0].ap(), U64.ap(), lav, start=True, stop=True),
                    reads=R("scb") + ("U64",), writes=(PSN[b0],))
                P.step("tensor", lambda e, lav=lav, b0=b0: e.matmul(
                    pbank[b0 + 1].ap(), Uc64.ap(), lav, start=True, stop=True),
                    reads=R("scb") + ("Uc64",), writes=(PSN[b0 + 1],))
                P.step("tensor", lambda e, lav=lav, b0=b0: e.matmul(
                    pbank[b0 + 2].ap()[0:2, :], Usum.ap(), lav, start=True, stop=True),
                    reads=R("scb") + ("Usum",), writes=(PSN[b0 + 2],))
            for tl in range(2):
                b0 = 3 * tl
                P.step("scalar", lambda e, tl=tl, b0=b0: e.activation(
                    sca[:, tl * 512:(tl + 1) * 512], pbank[b0].ap(), AF.Exp),
                    reads=(PSN[b0],), writes=R("sca"))
                P.step("scalar", lambda e, tl=tl, b0=b0: e.activation(
                    z_sb[:, tl * 512:(tl + 1) * 512], pbank[b0].ap(), AF.Exp, scale=-1.0),
                    reads=(PSN[b0],), writes=R(f"z{tl}"))
                P.step("scalar", lambda e, tl=tl, b0=b0: e.activation(
                    scb[:, tl * 512:(tl + 1) * 512], pbank[b0 + 1].ap(), AF.Exp),
                    reads=(PSN[b0 + 1],), writes=R("scb"))
                P.step("scalar", lambda e, tl=tl, b0=b0: e.activation(
                    dec2.ap()[0:2, tl * 512:(tl + 1) * 512], pbank[b0 + 2].ap()[0:2, :], AF.Exp),
                    reads=(PSN[b0 + 2],), writes=("dec2",))
            P.step("vector", lambda e: e.tensor_mul(q_sb, q_sb, sca),
                   reads=QB + R("sca"), writes=QB)
            P.step("vector", lambda e: e.tensor_mul(z_sb, k_sb, z_sb),
                   reads=KB + ZB, writes=ZB)
            P.step("vector", lambda e: e.tensor_mul(k_sb, k_sb, scb),
                   reads=KB + R("scb"), writes=KB)
            for tl in range(2):
                ch = hq * 2 + tl
                qv = q_sb[:, tl * 512:(tl + 1) * 512]
                kiv = z_sb[:, tl * 512:(tl + 1) * 512]
                ksv = k_sb[:, tl * 512:(tl + 1) * 512]
                vv_ = v_sb[:, tl * 512:(tl + 1) * 512]
                qres, kres, vres, zres = f"q{tl}", f"k{tl}", f"v{tl}", f"z{tl}"
                for h in range(H):
                    pb = 4 + h // 4
                    P.step("tensor", lambda e, h=h, pb=pb, qv=qv: e.transpose(
                        pbank[pb].ap()[0:64, (h % 4) * 128:(h % 4) * 128 + 128],
                        qv[:, h * DH:(h + 1) * DH], identf.ap()),
                        reads=R(qres) + ("identf",), writes=(PSN[pb],))
                for h in range(H):
                    pb = 6 + h // 4
                    P.step("tensor", lambda e, h=h, pb=pb, kiv=kiv: e.transpose(
                        pbank[pb].ap()[0:64, (h % 4) * 128:(h % 4) * 128 + 128],
                        kiv[:, h * DH:(h + 1) * DH], identf.ap()),
                        reads=R(zres) + ("identf",), writes=(PSN[pb],))
                P.step("vector", lambda e: e.tensor_copy(qT_sb[0:64, 0:512], pbank[4].ap()[0:64, :]),
                       reads=(PSN[4],), writes=R("qT"))
                P.step("vector", lambda e: e.tensor_copy(qT_sb[0:64, 512:1024], pbank[5].ap()[0:64, :]),
                       reads=(PSN[5],), writes=R("qT"))
                P.step("scalar", lambda e: e.copy(kT_sb[0:64, 0:512], pbank[6].ap()[0:64, :]),
                       reads=(PSN[6],), writes=R("kT"))
                P.step("scalar", lambda e: e.copy(kT_sb[0:64, 512:1024], pbank[7].ap()[0:64, :]),
                       reads=(PSN[7],), writes=R("kT"))
                for h in range(H):
                    pb = 4 + h % 2
                    P.step("tensor", lambda e, h=h, pb=pb: e.matmul(
                        pbank[pb].ap()[:, 0:128], kT_sb[0:64, h * 128:(h + 1) * 128],
                        qT_sb[0:64, h * 128:(h + 1) * 128], start=True, stop=True),
                        reads=R("kT", "qT"), writes=(PSN[pb],))
                    P.step("vector", lambda e, h=h, pb=pb: e.tensor_tensor(
                        PTm[:, (h % 2) * 128:(h % 2) * 128 + 128], pbank[pb].ap()[:, 0:128],
                        U64.ap(), op=ALU.mult),
                        reads=(PSN[pb], "U64"), writes=R("PTm"))
                    ob = 6 + h // 4
                    P.step("tensor", lambda e, h=h, ob=ob, vv_=vv_: e.matmul(
                        pbank[ob].ap()[0:64, (h % 4) * 128:(h % 4) * 128 + 128],
                        vv_[:, h * DH:(h + 1) * DH],
                        PTm[:, (h % 2) * 128:(h % 2) * 128 + 128],
                        start=(h % 4 == 0), stop=False),
                        reads=R(vres, "PTm"), writes=(PSN[ob],))
                    P.step("tensor", lambda e, h=h, ob=ob: e.matmul(
                        pbank[ob].ap()[0:64, (h % 4) * 128:(h % 4) * 128 + 64],
                        S_sb.ap()[0:64, h * DH:(h + 1) * DH],
                        qT_sb[0:64, h * 128:h * 128 + 64], start=False, stop=False),
                        reads=("S",) + R("qT"), writes=(PSN[ob],))
                db = 2 + 3 * tl
                for h in range(H):
                    P.step("tensor", lambda e, h=h, tl=tl, db=db: e.transpose(
                        pbank[db].ap()[0:64, h * 2:h * 2 + 2],
                        dec2.ap()[0:2, tl * 512 + h * DH: tl * 512 + (h + 1) * DH],
                        identf.ap()[0:2, 0:2]),
                        reads=("dec2", "identf"), writes=(PSN[db],))
                P.step("vector", lambda e, db=db: e.tensor_copy(dec_t.ap()[0:64, 0:16], pbank[db].ap()[0:64, 0:16]),
                       reads=(PSN[db],), writes=("dec_t",))
                for h in range(H):
                    P.step("tensor", lambda e, h=h, vv_=vv_, ksv=ksv: e.matmul(
                        pbank[3].ap()[0:64, h * DH:(h + 1) * DH],
                        ksv[0:64, h * DH:(h + 1) * DH],
                        vv_[0:64, h * DH:(h + 1) * DH], start=(h == 0), stop=(h == H - 1)),
                        reads=R(kres, vres), writes=(PSN[3],))
                P.step("vector", lambda e: e.tensor_tensor(
                    S_sb.ap().rearrange("p (h i) -> p h i", i=DH),
                    S_sb.ap().rearrange("p (h i) -> p h i", i=DH),
                    dec_t.ap().rearrange("p (h w) -> p h w", w=2)[:, :, 0:1].broadcast_to([64, H, DH]),
                    op=ALU.mult),
                    reads=("S", "dec_t"), writes=("S",))
                P.step("vector", lambda e: e.tensor_add(S_sb.ap(), S_sb.ap(), pbank[3].ap()[0:64, :]),
                       reads=("S", PSN[3]), writes=("S",))
                for h in range(H):
                    ob = 6 + h // 4
                    P.step("tensor", lambda e, h=h, ob=ob: e.matmul(
                        pbank[ob].ap()[0:64, (h % 4) * 128 + 64:(h % 4) * 128 + 128],
                        S_sb.ap()[0:64, h * DH:(h + 1) * DH],
                        qT_sb[0:64, h * 128 + 64:h * 128 + 128], start=False,
                        stop=(h % 4 == 3)),
                        reads=("S",) + R("qT"), writes=(PSN[ob],))
                for h in range(H):
                    P.step("tensor", lambda e, h=h, vv_=vv_, ksv=ksv: e.matmul(
                        pbank[3].ap()[0:64, h * DH:(h + 1) * DH],
                        ksv[64:128, h * DH:(h + 1) * DH],
                        vv_[64:128, h * DH:(h + 1) * DH], start=(h == 0), stop=(h == H - 1)),
                        reads=R(kres, vres), writes=(PSN[3],))
                P.step("vector", lambda e: e.tensor_tensor(
                    S_sb.ap().rearrange("p (h i) -> p h i", i=DH),
                    S_sb.ap().rearrange("p (h i) -> p h i", i=DH),
                    dec_t.ap().rearrange("p (h w) -> p h w", w=2)[:, :, 1:2].broadcast_to([64, H, DH]),
                    op=ALU.mult),
                    reads=("S", "dec_t"), writes=("S",))
                P.step("vector", lambda e: e.tensor_add(S_sb.ap(), S_sb.ap(), pbank[3].ap()[0:64, :]),
                       reads=("S", PSN[3]), writes=("S",))
                for h in range(H):
                    ob = 6 + h // 4
                    P.step("scalar", lambda e, h=h, ob=ob: e.copy(
                        oT_sb.ap()[(h % 2) * 64:(h % 2) * 64 + 64, (h // 2) * 128:(h // 2) * 128 + 128],
                        pbank[ob].ap()[0:64, (h % 4) * 128:(h % 4) * 128 + 128]),
                        reads=(PSN[ob],), writes=("oT",))
                for it in range(4):
                    for hf in range(2):
                        pb = 4 + hf
                        P.step("tensor", lambda e, it=it, hf=hf, pb=pb: e.matmul(
                            pbank[pb].ap(), oT_sb.ap()[:, it * 128:(it + 1) * 128],
                            aw_o.ap()[:, it * D + hf * 512: it * D + hf * 512 + 512],
                            start=(it == 0), stop=(it == 3)),
                            reads=("oT", "awo" + str(it)), writes=(PSN[pb],))
                x1v = x1.ap()[:, ch * 1024:(ch + 1) * 1024]
                x1res = f"x1t{ch}"
                xpuv = xpu[:, tl * 1024:(tl + 1) * 1024]
                P.step("vector", lambda e, x1v=x1v, xpuv=xpuv: e.tensor_add(x1v[:, 0:512], xpuv[:, 0:512], pbank[4].ap()),
                       reads=R("xpu") + (PSN[4],), writes=(x1res,))
                P.step("vector", lambda e, x1v=x1v, xpuv=xpuv: e.tensor_add(x1v[:, 512:1024], xpuv[:, 512:1024], pbank[5].ap()),
                       reads=R("xpu") + (PSN[5],), writes=(x1res,))

        def xn2_chain(ch):
            sl = 2 if ch % 2 == 0 else 0
            xn_t = xn2b[:, (ch % 2) * 1024:(ch % 2) * 1024 + 1024]
            xr = f"xn_t{ch % 2}"
            x1v = x1.ap()[:, ch * 1024:(ch + 1) * 1024]
            x1res = f"x1t{ch}"
            P.step("vector", lambda e, x1v=x1v, sl=sl: e.bn_stats(stats_s[sl].ap()[:, 0:6], x1v[:, 0:512]),
                   reads=(x1res,), writes=(f"stats{sl}",))
            P.step("vector", lambda e, x1v=x1v, sl=sl: e.bn_stats(stats_s[sl].ap()[:, 6:12], x1v[:, 512:1024]),
                   reads=(x1res,), writes=(f"stats{sl}",))
            P.step("vector", lambda e, sl=sl: e.bn_aggr(mv_s[sl].ap(), stats_s[sl].ap()),
                   reads=(f"stats{sl}",), writes=(f"mv{sl}",))
            P.step("scalar", lambda e, sl=sl: e.activation(col1_s[sl].ap(), mv_s[sl].ap()[:, 1:2], AF.Ln, bias=epsc.ap()),
                   reads=(f"mv{sl}", "epsc"), writes=(f"col1{sl}",))
            P.step("scalar", lambda e, sl=sl: e.activation(rstd_s[sl].ap(), col1_s[sl].ap(), AF.Exp, scale=-0.5),
                   reads=(f"col1{sl}",), writes=(f"rstd{sl}",))
            P.step("vector", lambda e, x1v=x1v, xn_t=xn_t, sl=sl: e.tensor_scalar(
                xn_t, x1v, mv_s[sl].ap()[:, 0:1], rstd_s[sl].ap(), op0=ALU.subtract, op1=ALU.mult),
                reads=(x1res, f"mv{sl}", f"rstd{sl}"), writes=R(xr))

        def xn2_emit(ch):
            xn_t = xn2b[:, (ch % 2) * 1024:(ch % 2) * 1024 + 1024]
            xr = f"xn_t{ch % 2}"
            for dt in range(NDT):
                pb = dt % 2
                rt = rot0.ap() if dt % 2 == 0 else sca[:, 0:128]
                rtres = "rot0" if dt % 2 == 0 else "sca"
                P.step("tensor", lambda e, dt=dt, pb=pb, xn_t=xn_t: e.transpose(
                    pbank[pb].ap()[:, 0:128], xn_t[:, dt * 128:(dt + 1) * 128], identf.ap()),
                    reads=R(xr) + ("identf",), writes=(PSN[pb],))
                P.step("scalar", lambda e, pb=pb, rt=rt: e.copy(rt, pbank[pb].ap()[:, 0:128]),
                       reads=(PSN[pb],), writes=(rtres,) if rtres == "rot0" else R(rtres))
                P.step("scalar", lambda e, dt=dt, pb=pb, ch=ch: e.copy(
                    xn2T.ap()[:, dt * 1024 + ch * 128: dt * 1024 + ch * 128 + 128],
                    pbank[pb].ap()[:, 0:128]),
                    reads=(PSN[pb],), writes=(f"xn2Tg{ch // 4}",))
                P.step("tensor", lambda e, dt=dt, rt=rt: e.matmul(
                    pbank[2].ap()[:, 0:E], rt, rw_sb.ap()[:, dt * E:(dt + 1) * E],
                    start=(dt == 0), stop=(dt == NDT - 1)),
                    reads=((rtres,) if rtres == "rot0" else R(rtres)) + (f"rw{dt}",), writes=(PSN[2],))
            P.step("vector", lambda e, ch=ch: e.tensor_copy(
                lgh.ap()[:, ch * E:(ch + 1) * E], pbank[2].ap()[:, 0:E]),
                reads=(PSN[2],), writes=("lgh",))

        xn2_chain(0)
        xn2_chain(1)
        for ch in range(8):
            xn2_emit(ch)
            if ch + 2 < 8:
                xn2_chain(ch + 2)

        L3 = lgh.ap().rearrange("p (c e) -> p c e", e=E)
        G3 = lg2.ap().rearrange("p (c e) -> p c e", e=E)
        B3 = lg2b.ap().rearrange("p (c e) -> p c e", e=E)
        mxb = mx8.ap().unsqueeze(2).broadcast_to([128, 8, E])
        s8b = s8.ap().unsqueeze(2).broadcast_to([128, 8, E])
        P.step("vector", lambda e: e.tensor_reduce(mx8.ap(), L3, axis=AX.X, op=ALU.max),
               reads=("lgh",), writes=("mx8",))
        P.step("vector", lambda e: e.tensor_tensor(G3, L3, mxb, op=ALU.subtract),
               reads=("lgh", "mx8"), writes=("lg2",))
        P.step("scalar", lambda e: e.activation(lg2b.ap(), lg2.ap(), AF.Exp),
               reads=("lg2",), writes=("lg2b",))
        P.step("vector", lambda e: e.tensor_reduce(s8.ap(), B3, axis=AX.X, op=ALU.add),
               reads=("lg2b",), writes=("s8",))
        P.step("vector", lambda e: e.reciprocal(s8.ap(), s8.ap()),
               reads=("s8",), writes=("s8",))
        P.step("vector", lambda e: e.tensor_scalar(lg2.ap(), lg2.ap(), 0.0, None, op0=ALU.is_equal),
               reads=("lg2",), writes=("lg2",))
        P.step("vector", lambda e: e.scalar_tensor_tensor(
            lg2.ap(), lg2.ap(), -100.0, iota8p.ap(), op0=ALU.mult, op1=ALU.add),
            reads=("lg2", "iota8p"), writes=("lg2",))
        P.step("vector", lambda e: e.tensor_reduce(mx8.ap(), G3, axis=AX.X, op=ALU.min),
               reads=("lg2",), writes=("mx8",))
        P.step("vector", lambda e: e.tensor_tensor(B3, iota8.ap().rearrange("p (c e) -> p c e", e=E), mxb, op=ALU.is_equal),
               reads=("iota8", "mx8"), writes=("lg2b",))
        P.step("vector", lambda e: e.tensor_tensor(
            G.ap().rearrange("p (c e) -> p c e", e=E), B3, s8b, op=ALU.mult),
            reads=("lg2b", "s8"), writes=("G",))
        for ch in range(8):
            gcol = G.ap()[:, ch * E:(ch + 1) * E]
            P.step("tensor", lambda e, gcol=gcol: e.transpose(
                pbank[3].ap()[0:E, 0:128], gcol, identf.ap()),
                reads=("G", "identf"), writes=(PSN[3],))
            P.step("vector", lambda e, ch=ch: e.tensor_copy(
                GT.ap()[0:E, ch * 128:(ch + 1) * 128], pbank[3].ap()[0:E, 0:128]),
                reads=(PSN[3],), writes=("GT",))

    def ffn_half(seq, half):
        def load_w1(e_i, quarter=None):
            g = P.new_group()
            dts = range(NDT) if quarter is None else range(quarter * 2, quarter * 2 + 2)
            for dt in dts:
                P.dma("gpsimd", lambda dt=dt: w1_sb[:, dt * 2048:(dt + 1) * 2048],
                      lambda dt=dt, e_i=e_i: w1_d[e_i, dt * 128:(dt + 1) * 128, :],
                      reads=("paceF2",) if quarter is not None else (),
                      writes=(f"w1d{dt}",), group=g)

        def load_w2(e_i, quarter=None):
            g = P.new_group()
            fts = range(NFT) if quarter is None else range(quarter * 4, quarter * 4 + 4)
            for ft in fts:
                P.dma("gpsimd", lambda ft=ft: w2_sb[:, ft * 1024:(ft + 1) * 1024],
                      lambda ft=ft, e_i=e_i: w2_d[e_i, ft * 128:(ft + 1) * 128, :],
                      reads=("paceF1",) if quarter is not None else (),
                      writes=(f"w2f{ft}",), group=g)

        load_w1(0)
        load_w2(0)
        for e_i in range(n_experts):
            for ft in range(NFT):
                for ng in range(2):
                    pb = (ft * 2 + ng) % 2
                    for dt in range(NDT):
                        P.step("tensor", lambda e, ft=ft, ng=ng, dt=dt, pb=pb: e.matmul(
                            pbank[pb].ap(),
                            w1_sb[:, dt * 2048 + ft * 128: dt * 2048 + ft * 128 + 128],
                            xn2T.ap()[:, dt * 1024 + ng * 512: dt * 1024 + ng * 512 + 512],
                            start=(dt == 0), stop=(dt == NDT - 1)),
                            reads=(f"w1d{dt}", f"xn2Tg{ng}"), writes=(PSN[pb],))
                    P.step("scalar", lambda e, ft=ft, ng=ng, pb=pb, e_i=e_i: e.activation(
                        hid[:, ft * 1024 + ng * 512: ft * 1024 + ng * 512 + 512],
                        pbank[pb].ap(), AF.Gelu,
                        bias=b1T.ap()[:, ft * E + e_i: ft * E + e_i + 1]),
                        reads=(PSN[pb], "b1T"), writes=(f"hidf{ft}",))
            if e_i + 1 < E:
                load_w1(e_i + 1)
            for nt in range(8):
                pb0, pb1 = (2, 3) if nt % 2 == 0 else (4, 5)
                for hf in range(2):
                    pb = pb0 if hf == 0 else pb1
                    for ft in range(NFT):
                        P.step("tensor", lambda e, nt=nt, hf=hf, ft=ft, pb=pb: e.matmul(
                            pbank[pb].ap(),
                            hid[:, ft * 1024 + nt * 128: ft * 1024 + nt * 128 + 128],
                            w2_sb[:, ft * 1024 + hf * 512: ft * 1024 + hf * 512 + 512],
                            start=(ft == 0), stop=(ft == NFT - 1)),
                            reads=(f"hidf{ft}", f"w2f{ft}"),
                            writes=(PSN[pb],) + (("paceF2",) if ft == NFT - 1 and hf == 1 else ()))
                gres = f"x1t{nt}"
                gsc = G.ap()[:, nt * E + e_i: nt * E + e_i + 1]
                x1v = x1.ap()[:, nt * 1024:(nt + 1) * 1024]
                P.step("vector", lambda e, x1v=x1v, pb0=pb0, gsc=gsc: e.scalar_tensor_tensor(
                    x1v[:, 0:512], pbank[pb0].ap(), gsc, x1v[:, 0:512], op0=ALU.mult, op1=ALU.add),
                    reads=(PSN[pb0], "G", gres), writes=(gres,))
                P.step("vector", lambda e, x1v=x1v, pb1=pb1, gsc=gsc: e.scalar_tensor_tensor(
                    x1v[:, 512:1024], pbank[pb1].ap(), gsc, x1v[:, 512:1024], op0=ALU.mult, op1=ALU.add),
                    reads=(PSN[pb1], "G", gres), writes=(gres,))
            if e_i + 1 < E:
                load_w2(e_i + 1)
        for nt in range(8):
            pb0, pb1 = (2, 3) if nt % 2 == 0 else (4, 5)
            for hf in range(2):
                pb = pb0 if hf == 0 else pb1
                P.step("tensor", lambda e, nt=nt, hf=hf, pb=pb: e.matmul(
                    pbank[pb].ap(), GT.ap()[0:E, nt * 128:(nt + 1) * 128],
                    b2_sb.ap()[0:E, hf * 512:(hf + 1) * 512], start=True, stop=True),
                    reads=("GT", "b2"), writes=(PSN[pb],))
            gres = f"x1t{nt}"
            x1v = x1.ap()[:, nt * 1024:(nt + 1) * 1024]
            P.step("vector", lambda e, x1v=x1v, pb0=pb0: e.tensor_tensor(
                x1v[:, 0:512], x1v[:, 0:512], pbank[pb0].ap(), op=ALU.add),
                reads=(gres, PSN[pb0]), writes=(gres,))
            P.step("vector", lambda e, x1v=x1v, pb1=pb1: e.tensor_tensor(
                x1v[:, 512:1024], x1v[:, 512:1024], pbank[pb1].ap(), op=ALU.add),
                reads=(gres, PSN[pb1]), writes=(gres,))
        P.dma("sync", lambda seq=seq, half=half: y_d[seq, half * 1024:(half + 1) * 1024, :]
              .rearrange("(a p) n -> p a n", p=128),
              lambda: x1.ap().rearrange("p (a n) -> p a n", n=D),
              reads=tuple(f"x1t{i}" for i in range(8)), writes=("ydone",))

    _halves = 2 * n_seq if n_halves is None else n_halves
    cnt_h = 0
    for seq in range(n_seq):
        for half in range(2):
            if cnt_h >= _halves:
                break
            cnt_h += 1
            if half == 0:
                P.step("vector", lambda e: e.memset(S_sb.ap(), 0.0),
                       reads=(), writes=("S",))
            attention_half(seq, half)
            ffn_half(seq, half)

    waits, incs = P.resolve()
    sem_guards = []
    sems = {}
    for e in SEMS:
        g = nc.semaphore("sem_" + e)
        sems[e] = g.__enter__()
        sem_guards.append(g)
    total = {e: 0 for e in SEMS}
    for i, st in enumerate(P.steps):
        total[sem_of(st[0], st[4])] += incs[i]

    blk = nc.Block()
    block = blk.__enter__()

    def mk(eng_name):
        def body(eng):
            for i in P.order[eng_name]:
                fn = P.steps[i][1]
                for (src_eng, v) in waits[i]:
                    eng.wait_ge(sems[src_eng], v)
                instr = fn(eng)
                if incs[i]:
                    instr.then_inc(sems[sem_of(eng_name, P.steps[i][4])], incs[i])
            if eng_name == "sync":
                eng.wait_ge(sems["sync"], total["sync"])
        return body

    block.sync(mk("sync"))
    block.tensor(mk("tensor"))
    block.vector(mk("vector"))
    block.scalar(mk("scalar"))
    block.gpsimd(mk("gpsimd"))
    blk.__exit__(None, None, None)
    for g in reversed(sem_guards):
        g.__exit__(None, None, None)
    return nc


def kernel(**inputs):
    n_cores = 8
    x = np.ascontiguousarray(inputs["L0_attn_x"])
    shards = np.split(x, n_cores, axis=0)
    nc = build(n_seq=shards[0].shape[0])
    common = {k: np.ascontiguousarray(np.asarray(v, dtype=np.float32))
              for k, v in inputs.items() if k != "L0_attn_x"}
    in_maps = [dict(common, L0_attn_x=np.ascontiguousarray(s)) for s in shards]
    res = run_bass_kernel_spmd(nc, in_maps, core_ids=list(range(n_cores)))
    return np.concatenate([r["y"] for r in res.results], axis=0)
